# version 10
# baseline (speedup 1.0000x reference)
from contextlib import ExitStack

import numpy as np

import concourse.bass as bass
import concourse.mybir as mybir
from concourse.bass_isa import ReduceOp
from concourse.bass_utils import run_bass_kernel_spmd

F32 = mybir.dt.float32
BF16 = mybir.dt.bfloat16
AF = mybir.ActivationFunctionType
ALU = mybir.AluOpType
AX = mybir.AxisListType

D = 1024
T = 2048
DI = 2048
NS = 16
ND = D // 128
NDI = DI // 128
FF = 4096
TC = 256
NCH = T // TC
TCA = 256
TCF = 512
DCOLS = 128
SG = 4
MAGIC = 12582912.0
NLAYER = 4
_NL = [4]
DBGNOCARRY = False
DBGNOLORA = False
DBGNOQUANT = False
DBGQUANTP1ONLY = False
DBGQUANTSTAGE = 0

C_BZ = 0
C_BH = 8
C_DCOEF = 24
C_BFC = 32
C_BPJ = 64
C_EPS = 72
C_ONE = 73
C_IFC = 128
C_SFC = 136
C_IPJ = 160
C_SPJ = 168
C_127 = 184
C_R127 = 185
C_INVD = 79
C_CB = 80
C_DTB = 96
C_DSK = 112
C_NEG1 = 176
C_LOGD = 144


def build(upto="full"):
    nc = bass.Bass("TRN2", target_bir_lowering=False, debug=False)

    def di(name, shape, dt=F32):
        return nc.dram_tensor(name, shape, dt, kind="ExternalInput").ap()

    def dint(name, shape, dt):
        return nc.dram_tensor(name, shape, dt, kind="Internal").ap()

    d_x = di("x", [D, T])
    d_wz = di("wz", [D, D])
    d_wh = di("wh", [D, D])
    d_bz = di("bz", [128, ND])
    d_bh = di("bh", [128, ND])
    d_logd = di("logd", [128, ND])
    d_wfcT = di("wfcT", [D, FF])
    d_wprojT = di("wprojT", [FF, D])
    d_afcT = di("afcT", [D, 16])
    d_bfcT = di("bfcT", [16, FF])
    d_aprojT = di("aprojT", [FF, 16])
    d_bprojT = di("bprojT", [16, D])
    d_bfc = di("bfc", [128, 32])
    d_bproj = di("bproj", [128, ND])
    d_inwT = [di(f"inwT{l}", [D, 2 * DI]) for l in range(NLAYER)]
    d_xpwT = [di(f"xpwT{l}", [DI, 96]) for l in range(NLAYER)]
    d_dtwT = [di(f"dtwT{l}", [64, DI]) for l in range(NLAYER)]
    d_opwT = [di(f"opwT{l}", [DI, D]) for l in range(NLAYER)]
    d_cw = [di(f"cw{l}", [128, NDI * 4]) for l in range(NLAYER)]
    d_cb = [di(f"cb{l}", [128, NDI]) for l in range(NLAYER)]
    d_dtb = [di(f"dtb{l}", [128, NDI]) for l in range(NLAYER)]
    d_alog = [di(f"alog{l}", [128, NDI * NS]) for l in range(NLAYER)]
    d_dsk = [di(f"dsk{l}", [128, NDI]) for l in range(NLAYER)]

    d_y = nc.dram_tensor("y", [D, T], F32, kind="ExternalOutput").ap()

    d_xspill = dint("xspill", [D, T], F32)
    d_xafter = dint("xafter", [D, T], F32)
    d_wqfcB = dint("wqfcB", [32 * 128, 8 * 128], BF16)
    d_wqpjB = dint("wqpjB", [8 * 128, 32 * 128], BF16)
    d_inwB = [dint(f"inwB{l}", [32 * 128, 8 * 128], BF16) for l in range(NLAYER)]
    d_opwB = [dint(f"opwB{l}", [8 * 128, 16 * 128], BF16) for l in range(NLAYER)]
    d_xpwB = [dint(f"xpwB{l}", [DI, 96], BF16) for l in range(NLAYER)]
    d_dtwB = [dint(f"dtwB{l}", [64, DI], BF16) for l in range(NLAYER)]

    ctx = ExitStack()
    ec = ctx.enter_context

    sems = {}
    for nm in ("LD", "GD", "SD", "T", "S", "V"):
        sems[nm] = ec(nc.semaphore(nm))
    cnt = {nm: 0 for nm in sems}

    def inc(instr, nm, n=1):
        instr.then_inc(sems[nm], n)
        cnt[nm] += n
        return cnt[nm]

    def wait(eng, nm, val=None):
        eng.wait_ge(sems[nm], cnt[nm] if val is None else val)

    xnb = ec(nc.sbuf_tensor("xnb", [128, ND * T], BF16))
    colsF = ec(nc.sbuf_tensor("colsF", [128, 192], F32))
    acolF = ec(nc.sbuf_tensor("acolF", [128, NDI * NS], F32))
    cwF = ec(nc.sbuf_tensor("cwF", [128, NDI * 4], F32))
    selB = ec(nc.sbuf_tensor("selB", [16, 16 * 128], BF16))
    onesB = ec(nc.sbuf_tensor("onesB", [128, 128], BF16))
    afcB = ec(nc.sbuf_tensor("afcB", [128, ND * 16], BF16))
    bfc2B = ec(nc.sbuf_tensor("bfc2B", [16, FF], BF16))
    aprojB = ec(nc.sbuf_tensor("aprojB", [128, 32 * 16], BF16))
    bproj2B = ec(nc.sbuf_tensor("bproj2B", [16, D], BF16))
    mcol = ec(nc.sbuf_tensor("mcol", [128, 48], F32))
    qsc = ec(nc.sbuf_tensor("qsc", [128, 512], F32))

    x1_stack = ExitStack()
    X1 = x1_stack.enter_context(nc.sbuf_tensor("X1", [128, ND * T], F32))

    psA = ec(nc.psum_tensor("psA", [128, 512], F32))
    psB = ec(nc.psum_tensor("psB", [128, 512], F32))
    psC = ec(nc.psum_tensor("psC", [128, 512], F32))
    psD = ec(nc.psum_tensor("psD", [128, 512], F32))
    psS96 = ec(nc.psum_tensor("psS96", [96, 512], F32))
    psL16 = ec(nc.psum_tensor("psL16", [16, 512], F32))
    psL16b = ec(nc.psum_tensor("psL16b", [16, 512], F32))
    psX = ec(nc.psum_tensor("psX", [64, 64], F32))

    blk = ec(nc.Block())

    def tdrain(tE):
        return tE.matmul(psX[:, 0:64], onesB[:, 0:64], onesB[:, 64:128],
                         start=True, stop=True)

    def barrier():
        tgt = dict(cnt)
        for dec, own in ((blk.tensor, "T"), (blk.scalar, "S"), (blk.vector, "V"),
                         (blk.sync, None), (blk.gpsimd, None)):
            def mk(own=own):
                def f(e):
                    for nm in ("T", "S", "V", "LD", "GD", "SD"):
                        if nm != own:
                            e.wait_ge(sems[nm], tgt[nm])
                return f
            dec(mk())

    def mm_stage(items, banks, evac_eng="S"):
        nb = len(banks)
        t0 = cnt["T"]
        e0 = cnt[evac_eng]
        ld0 = cnt["LD"]
        nload = [0] * len(items)
        if any(it.get("load") for it in items):
            @blk.sync
            def _(q):
                for i, it in enumerate(items):
                    if not it.get("load"):
                        continue
                    if i >= 2:
                        q.wait_ge(sems["T"], t0 + i - 1)
                    nload[i] = it["load"](q)
                    cnt["LD"] += 16 * nload[i]
        cum = ld0
        cumload = []
        for n in nload:
            cum += 16 * n
            cumload.append(cum)

        sS0 = cnt["S"]
        sV0 = cnt["V"]

        @blk.tensor
        def _(tE):
            tE.wait_ge(sems["S"], sS0)
            tE.wait_ge(sems["V"], sV0)
            for i, it in enumerate(items):
                if nload[i]:
                    tE.wait_ge(sems["LD"], cumload[i])
                if i >= nb:
                    tE.wait_ge(sems[evac_eng], e0 + i - nb + 1)
                last = it["mms"](tE, banks[i % nb])
                last.then_inc(sems["T"], 1)
                cnt["T"] += 1
            last = tdrain(tE)
            last.then_inc(sems["T"], 1)
            cnt["T"] += 1

        dec = blk.scalar if evac_eng == "S" else blk.vector

        @dec
        def _(eE):
            for i, it in enumerate(items):
                eE.wait_ge(sems["T"], t0 + i + 2)
                last = it["evac"](eE, banks[i % nb])
                last.then_inc(sems[evac_eng], 1)
                cnt[evac_eng] += 1

    def emit_ln(src_ap, scr):
        xb, sqb, mb, rb, tb = scr
        for c0 in range(0, T, 512):
            cs = slice(c0, c0 + 512)

            @blk.scalar
            def _(s):
                wait(s, "V")
                wait(s, "T")
                for j in range(ND):
                    s.copy(xb[:, j * 512:(j + 1) * 512], src_ap(j, cs))
                for j in range(ND):
                    last = s.activation(sqb[:, j * 512:(j + 1) * 512],
                                        xb[:, j * 512:(j + 1) * 512], AF.Square)
                inc(last, "S")

            @blk.tensor
            def _(tE):
                wait(tE, "S")
                for j in range(ND):
                    tE.matmul(psA[:], onesB[:], xb[:, j * 512:(j + 1) * 512],
                              start=(j == 0), stop=(j == ND - 1))
                for j in range(ND):
                    tE.matmul(psB[:], onesB[:], sqb[:, j * 512:(j + 1) * 512],
                              start=(j == 0), stop=(j == ND - 1))
                inc(tdrain(tE), "T")

            @blk.scalar
            def _(s):
                wait(s, "T")
                last = s.activation(mb[:], psA[:], AF.Copy,
                                    scale=colsF[:, C_INVD:C_INVD + 1])
                inc(last, "S")

            @blk.vector
            def _(v):
                wait(v, "S")
                v.tensor_tensor(tb[:], mb[:], mb[:], ALU.mult)
                last = v.scalar_tensor_tensor(tb[:], psB[:],
                                              colsF[:, C_INVD:C_INVD + 1], tb[:],
                                              ALU.mult, ALU.subtract)
                inc(last, "V")

            @blk.scalar
            def _(s):
                wait(s, "V")
                last = s.activation(rb[:], tb[:], AF.Sqrt,
                                    bias=colsF[:, C_EPS:C_EPS + 1])
                inc(last, "S")

            @blk.vector
            def _(v):
                wait(v, "S")
                v.reciprocal(rb[:], rb[:])
                for j in range(ND):
                    v.scalar_tensor_tensor(tb[:], mb[:], -1.0, src_ap(j, cs),
                                           ALU.mult, ALU.add)
                    last = v.tensor_tensor(xnb[:, j * T + c0:j * T + c0 + 512],
                                           tb[:], rb[:], ALU.mult)
                inc(last, "V")

    with ExitStack() as p0:
        pr = p0.enter_context
        stgF = pr(nc.sbuf_tensor("stgF", [128, 2 * FF], F32, side="right"))
        stgB = pr(nc.sbuf_tensor("stgB", [128, 2 * FF], BF16, side="right"))
        qb = pr(nc.sbuf_tensor("qb", [128, FF], F32, side="right"))
        qbf = pr(nc.sbuf_tensor("qbf", [128, FF], BF16, side="right"))

        @blk.sync
        def _(q):
            for j in range(ND):
                q.dma_start(out=X1[:, j * T:(j + 1) * T],
                            in_=d_x[j * 128:(j + 1) * 128, :]).then_inc(sems["LD"], 16)
            q.dma_start(out=colsF[:, C_BZ:C_BZ + 8], in_=d_bz).then_inc(sems["LD"], 16)
            q.dma_start(out=colsF[:, C_BH:C_BH + 8], in_=d_bh).then_inc(sems["LD"], 16)
            q.dma_start(out=colsF[:, C_LOGD:C_LOGD + 8], in_=d_logd).then_inc(sems["LD"], 16)
            q.dma_start(out=colsF[:, C_BFC:C_BFC + 32], in_=d_bfc).then_inc(sems["LD"], 16)
            q.dma_start(out=colsF[:, C_BPJ:C_BPJ + 8], in_=d_bproj).then_inc(sems["LD"], 16)
            cnt["LD"] += 16 * (ND + 5)

        @blk.gpsimd
        def _(g):
            for j in range(ND):
                g.dma_start(out=afcB[:, j * 16:(j + 1) * 16],
                            in_=d_afcT[j * 128:(j + 1) * 128, :]).then_inc(sems["GD"], 16)
            for j in range(32):
                g.dma_start(out=aprojB[:, j * 16:(j + 1) * 16],
                            in_=d_aprojT[j * 128:(j + 1) * 128, :]).then_inc(sems["GD"], 16)
            cnt["GD"] += 16 * (ND + 32)

        @blk.vector
        def _(v):
            v.memset(colsF[:, C_EPS:C_EPS + 1], 1e-5)
            v.memset(colsF[:, C_ONE:C_ONE + 1], 1.0)
            v.memset(colsF[:, C_INVD:C_INVD + 1], 1.0 / D)
            v.memset(colsF[:, C_NEG1:C_NEG1 + 1], -1.0)
            v.memset(colsF[:, C_127:C_127 + 1], 127.0)
            v.memset(colsF[:, C_R127:C_R127 + 1], 1.0 / 127.0)
            v.memset(onesB[:], 1.0)
            v.memset(selB[:], 0.0)
            wait(v, "LD")
            last = v.tensor_copy(colsF[:, C_DCOEF:C_DCOEF + 8],
                                 colsF[:, C_LOGD:C_LOGD + 8])
            inc(last, "V")

        @blk.scalar
        def _(s):
            wait(s, "V")
            last = s.activation(colsF[:, C_DCOEF:C_DCOEF + 8],
                                colsF[:, C_DCOEF:C_DCOEF + 8], AF.Exp)
            inc(last, "S")

        @blk.sync
        def _(q):
            wait(q, "V")
            for s_ in range(16):
                q.dma_start(out=selB[s_:s_ + 1, s_ * 128:(s_ + 1) * 128],
                            in_=onesB[0:1, 0:128]).then_inc(sems["LD"], 16)
            cnt["LD"] += 16 * 16

        @blk.sync
        def _(q):
            q.dma_start(out=stgF[:16, 0:FF], in_=d_bfcT).then_inc(sems["LD"], 16)
            q.dma_start(out=stgF[:16, FF:FF + D], in_=d_bprojT).then_inc(sems["LD"], 16)
            cnt["LD"] += 32

        @blk.vector
        def _(v):
            wait(v, "LD")
            v.tensor_scalar(bfc2B[:], stgF[:16, 0:FF], 2.0, None, ALU.mult)
            last = v.tensor_scalar(bproj2B[:], stgF[:16, FF:FF + D], 2.0,
                                   None, ALU.mult)
            inc(last, "V")
        barrier()

        def quant(src, nptile, nslab, blkdst, qcol):
            w = nslab * 128
            ventry = cnt["V"]
            ld0 = cnt["LD"]
            v0 = cnt["V"]

            @blk.sync
            def _(q):
                for k in range(nptile):
                    if k < 2:
                        q.wait_ge(sems["V"], ventry)
                    else:
                        q.wait_ge(sems["V"], v0 + k - 1)
                    q.dma_start(out=stgF[:, (k % 2) * w:(k % 2) * w + w],
                                in_=src[k * 128:(k + 1) * 128, :]).then_inc(sems["LD"], 16)
                cnt["LD"] += 16 * nptile

            @blk.vector
            def _(v):
                for k in range(nptile):
                    v.wait_ge(sems["LD"], ld0 + 16 * (k + 1))
                    last = v.tensor_reduce(mcol[:, k:k + 1],
                                           stgF[:, (k % 2) * w:(k % 2) * w + w],
                                           AX.X, ALU.max,
                                           apply_absolute_value=True)
                    last.then_inc(sems["V"], 1)
                    cnt["V"] += 1
                    v.wait_ge(sems["V"], cnt["V"])
                last = v.tensor_reduce(mcol[:, 32:33], mcol[:, 0:nptile],
                                       AX.X, ALU.max)
                inc(last, "V")
                wait(v, "V")

            @blk.sync
            def _(q):
                wait(q, "V")
                q.dma_start(out=qb[0:1, 0:128],
                            in_=mcol[:, 32:33]).then_inc(sems["LD"], 16)
                cnt["LD"] += 16

            @blk.vector
            def _(v):
                wait(v, "LD")
                v.memset(qb[0:1, 256:384], 1.0)
                last = v.tensor_reduce(qb[0:1, 128:129], qb[0:1, 0:128],
                                       AX.X, ALU.max)
                inc(last, "V")
                wait(v, "V")
                bsc = qb[0:1, 128:129].broadcast_to([1, 128])
                last = v.tensor_tensor(qb[0:1, 384:512], bsc, bsc, ALU.max)
                inc(last, "V")

            @blk.tensor
            def _(tE):
                wait(tE, "V")
                tE.matmul(psA[:, 0:128], qb[0:1, 256:384],
                          qb[0:1, 384:512], start=True, stop=True)
                inc(tdrain(tE), "T")

            @blk.vector
            def _(v):
                wait(v, "T")
                negb = colsF[:, C_NEG1:C_NEG1 + 1].broadcast_to([128, 128])
                q0 = qcol
                v.tensor_tensor(qsc[:, q0:q0 + 128], psA[:, 0:128], negb,
                                ALU.max)
                v.reciprocal(qsc[:, q0 + 128:q0 + 256], qsc[:, q0:q0 + 128])
                v.tensor_tensor(
                    qsc[:, q0 + 128:q0 + 256], qsc[:, q0 + 128:q0 + 256],
                    colsF[:, C_127:C_127 + 1].broadcast_to([128, 128]),
                    ALU.mult)
                last = v.tensor_tensor(
                    qsc[:, q0:q0 + 128], qsc[:, q0:q0 + 128],
                    colsF[:, C_R127:C_R127 + 1].broadcast_to([128, 128]),
                    ALU.mult)
                inc(last, "V")
                wait(v, "V")

            if DBGQUANTP1ONLY:
                return
            ld1 = cnt["LD"]
            v1 = cnt["V"]
            sd0 = cnt["SD"]

            @blk.sync
            def _(q):
                for k in range(nptile):
                    if k < 2:
                        q.wait_ge(sems["V"], v1)
                    else:
                        q.wait_ge(sems["V"], v1 + 3 * (k - 1))
                    q.dma_start(out=stgF[:, (k % 2) * w:(k % 2) * w + w],
                                in_=src[k * 128:(k + 1) * 128, :]).then_inc(sems["LD"], 16)
                cnt["LD"] += 16 * nptile

            @blk.vector
            def _(v):
                v.wait_ge(sems["V"], v1)
                for k in range(nptile):
                    v.wait_ge(sems["LD"], ld1 + 16 * (k + 1))
                    if k >= 1:
                        v.wait_ge(sems["SD"], sd0 + 16 * nslab * k)
                    sl = stgF[:, (k % 2) * w:(k % 2) * w + w]
                    v.tensor_scalar(qb[:, 0:w], sl,
                                    qsc[:, qcol + 128:qcol + 129],
                                    MAGIC, ALU.mult, ALU.add)
                    v.tensor_scalar(qb[:, 0:w], qb[:, 0:w], MAGIC, -128.0,
                                    ALU.subtract, ALU.max)
                    last = v.tensor_scalar(qbf[:, 0:w], qb[:, 0:w], 127.0,
                                           qsc[:, qcol:qcol + 1],
                                           ALU.min, ALU.mult)
                    last.then_inc(sems["V"], 3)
                    cnt["V"] += 3

            @blk.scalar
            def _(s):
                for k in range(nptile):
                    s.wait_ge(sems["V"], v1 + 3 * (k + 1))
                    for m in range(nslab):
                        s.dma_start(out=blkdst[m * 128:(m + 1) * 128,
                                               k * 128:(k + 1) * 128],
                                    in_=qbf[:, m * 128:(m + 1) * 128]
                                    ).then_inc(sems["SD"], 16)
                    cnt["SD"] += 16 * nslab

        if not DBGNOQUANT:
            quant(d_wfcT, 8, 32, d_wqfcB, 0)
            quant(d_wprojT, 32, 8, d_wqpjB, 256)

        jobs = []
        for l in range(NLAYER):
            for k in range(8):
                jobs.append((d_inwT[l][k * 128:(k + 1) * 128, :], 128, 4096,
                             [(d_inwB[l][m * 128:(m + 1) * 128,
                                         k * 128:(k + 1) * 128], m * 128)
                              for m in range(32)]))
            for k in range(16):
                jobs.append((d_opwT[l][k * 128:(k + 1) * 128, :], 128, 1024,
                             [(d_opwB[l][m * 128:(m + 1) * 128,
                                         k * 128:(k + 1) * 128], m * 128)
                              for m in range(8)]))
            for k in range(16):
                jobs.append((d_xpwT[l][k * 128:(k + 1) * 128, :], 128, 96,
                             [(d_xpwB[l][k * 128:(k + 1) * 128, :], 0)]))
            jobs.append((d_dtwT[l], 64, 2048, [(d_dtwB[l], 0)]))

        gd0 = cnt["GD"]
        sd0 = cnt["SD"]
        cum_sd = []
        acc = sd0
        for (_, _, _, outs) in jobs:
            acc += 16 * len(outs)
            cum_sd.append(acc)

        @blk.gpsimd
        def _(g):
            for i, (srcap, p, w, outs) in enumerate(jobs):
                if i >= 2:
                    g.wait_ge(sems["SD"], cum_sd[i - 2])
                g.dma_start(out=stgB[0:p, (i % 2) * FF:(i % 2) * FF + w],
                            in_=srcap).then_inc(sems["GD"], 16)
                cnt["GD"] += 16

        @blk.scalar
        def _(s):
            for i, (srcap, p, w, outs) in enumerate(jobs):
                s.wait_ge(sems["GD"], gd0 + 16 * (i + 1))
                for dst, c0 in outs:
                    ww = dst.shape[1]
                    s.dma_start(out=dst,
                                in_=stgB[0:p, (i % 2) * FF + c0:
                                         (i % 2) * FF + c0 + ww]
                                ).then_inc(sems["SD"], 16)
                cnt["SD"] += 16 * len(outs)
        barrier()

    if upto == "p0":
        @blk.sync
        def _(q):
            for nm in ("T", "S", "V", "GD", "SD"):
                q.wait_ge(sems[nm], cnt[nm])
            q.dma_start(out=d_y[0:128, 0:192], in_=colsF[:]).then_inc(sems["LD"], 16)
            cnt["LD"] += 16
            wait(q, "LD")

        @blk.gpsimd
        def _(g):
            wait(g, "LD")
            g.dma_start(out=d_y[0:128, 192:1216],
                        in_=d_wqfcB[0:128, :]).then_inc(sems["GD"], 16)
            g.dma_start(out=d_y[0:128, 1216:1312],
                        in_=d_xpwB[0][0:128, :]).then_inc(sems["GD"], 16)
            g.dma_start(out=d_y[128:256, 0:1024],
                        in_=d_inwB[0][0:128, :]).then_inc(sems["GD"], 16)
            g.dma_start(out=d_y[256:384, 0:2048],
                        in_=d_opwB[0][0:128, :]).then_inc(sems["GD"], 16)
            cnt["GD"] += 64
            wait(g, "GD")
        x1_stack.close()
        ctx.close()
        return nc

    with ExitStack() as p1:
        pr = p1.enter_context
        wzB = pr(nc.sbuf_tensor("wzB", [128, ND * D], BF16, side="right"))
        whB = pr(nc.sbuf_tensor("whB", [128, ND * D], BF16, side="right"))

        @blk.gpsimd
        def _(g):
            for j in range(ND):
                g.dma_start(out=wzB[:, j * D:(j + 1) * D],
                            in_=d_wz[j * 128:(j + 1) * 128, :]).then_inc(sems["GD"], 16)
                g.dma_start(out=whB[:, j * D:(j + 1) * D],
                            in_=d_wh[j * 128:(j + 1) * 128, :]).then_inc(sems["GD"], 16)
            cnt["GD"] += 16 * 2 * ND

        with ExitStack() as lnscope:
            lr = lnscope.enter_context
            scr = (lr(nc.sbuf_tensor("a_xb", [128, ND * 512], BF16, side="right")),
                   lr(nc.sbuf_tensor("a_sq", [128, ND * 512], BF16, side="right")),
                   lr(nc.sbuf_tensor("a_m", [128, 512], F32, side="right")),
                   lr(nc.sbuf_tensor("a_r", [128, 512], F32, side="right")),
                   lr(nc.sbuf_tensor("a_t", [128, 512], F32, side="right")))
            emit_ln(lambda j, cs: X1[:, j * T:(j + 1) * T][:, cs], scr)
            barrier()

        HH = 4
        TCH = TCA + HH
        zb = pr(nc.sbuf_tensor("zb", [128, ND * TCA], F32, side="right"))
        htb = pr(nc.sbuf_tensor("htb", [128, ND * TCA], F32, side="right"))
        asb = pr(nc.sbuf_tensor("asb", [128, ND * TCH], F32, side="right"))
        bsb = pr(nc.sbuf_tensor("bsb", [128, ND * TCH], F32, side="right"))
        hcc = pr(nc.sbuf_tensor("hcc", [128, ND * TCH], F32, side="right"))
        hlb = pr(nc.sbuf_tensor("hlb", [128, ND * DCOLS], F32, side="right"))
        hrb = pr(nc.sbuf_tensor("hrb", [128, ND * DCOLS], F32, side="right"))
        dfb = pr(nc.sbuf_tensor("dfb", [128, DCOLS], F32, side="right"))

        @blk.vector
        def _(v):
            v.memset(asb[:], 0.0)
            last = v.memset(bsb[:], 0.0)
            inc(last, "V")

        for cc in range(T // TCA):
            cs = slice(cc * TCA, (cc + 1) * TCA)
            items = []
            for j in range(ND):
                def mk_mm(j=j, cs=cs):
                    def f(tE, ps):
                        for k in range(ND):
                            last = tE.matmul(
                                ps[:, 0:TCA],
                                wzB[:, k * D + j * 128:k * D + (j + 1) * 128],
                                xnb[:, k * T:(k + 1) * T][:, cs],
                                start=(k == 0), stop=(k == ND - 1))
                        return last
                    return f

                def mk_ev(j=j):
                    def f(sE, ps):
                        return sE.activation(zb[:, j * TCA:(j + 1) * TCA],
                                             ps[:, 0:TCA], AF.Sigmoid,
                                             bias=colsF[:, C_BZ + j:C_BZ + j + 1])
                    return f
                items.append({"mms": mk_mm(), "evac": mk_ev()})
            for j in range(ND):
                def mk_mm(j=j, cs=cs):
                    def f(tE, ps):
                        for k in range(ND):
                            last = tE.matmul(
                                ps[:, 0:TCA],
                                whB[:, k * D + j * 128:k * D + (j + 1) * 128],
                                xnb[:, k * T:(k + 1) * T][:, cs],
                                start=(k == 0), stop=(k == ND - 1))
                        return last
                    return f

                def mk_ev(j=j):
                    def f(sE, ps):
                        return sE.activation(htb[:, j * TCA:(j + 1) * TCA],
                                             ps[:, 0:TCA], AF.Identity,
                                             bias=colsF[:, C_BH + j:C_BH + j + 1])
                    return f
                items.append({"mms": mk_mm(), "evac": mk_ev()})
            if cc == 0:
                @blk.tensor
                def _(tE):
                    wait(tE, "GD")
            mm_stage(items, [psA, psB])
            barrier()

            @blk.vector
            def _(v):
                for j in range(ND):
                    sj = slice(j * TCA, (j + 1) * TCA)
                    b0 = j * TCH
                    v.tensor_scalar(asb[:, b0 + HH:b0 + TCH], zb[:, sj],
                                    -1.0, 1.0, ALU.mult, ALU.add)
                    last = v.tensor_tensor(bsb[:, b0 + HH:b0 + TCH], zb[:, sj],
                                           htb[:, sj], ALU.mult)
                    inc(last, "V")
                    wait(v, "V")
                    last = v.tensor_tensor_scan(hcc[:, b0:b0 + TCH],
                                                asb[:, b0:b0 + TCH],
                                                bsb[:, b0:b0 + TCH],
                                                0.0, ALU.mult, ALU.add)
                    inc(last, "V")
                    wait(v, "V")
                    lc = hcc[:, b0 + TCH - HH:b0 + TCH]
                    last = v.tensor_tensor(bsb[:, b0:b0 + HH], lc, lc, ALU.max)
                inc(last, "V")
            barrier()

            for d0 in range(0, TCA, DCOLS):
                @blk.sync
                def _(q):
                    wait(q, "V")
                    for j in range(ND):
                        sj = slice(j * TCH + HH + d0, j * TCH + HH + d0 + DCOLS)
                        jm, jp = (j - 1) % ND, (j + 1) % ND
                        sjm = slice(jm * TCH + HH + d0, jm * TCH + HH + d0 + DCOLS)
                        sjp = slice(jp * TCH + HH + d0, jp * TCH + HH + d0 + DCOLS)
                        dj = slice(j * DCOLS, (j + 1) * DCOLS)
                        q.dma_start(out=hlb[:, dj][1:128, :],
                                    in_=hcc[:, sj][0:127, :]).then_inc(sems["LD"], 16)
                        q.dma_start(out=hlb[:, dj][0:1, :],
                                    in_=hcc[:, sjm][127:128, :]).then_inc(sems["LD"], 16)
                        q.dma_start(out=hrb[:, dj][0:127, :],
                                    in_=hcc[:, sj][1:128, :]).then_inc(sems["LD"], 16)
                        q.dma_start(out=hrb[:, dj][127:128, :],
                                    in_=hcc[:, sjp][0:1, :]).then_inc(sems["LD"], 16)
                    cnt["LD"] += 16 * 4 * ND

                @blk.vector
                def _(v):
                    wait(v, "LD")
                    for j in range(ND):
                        sj = slice(j * TCH + HH + d0, j * TCH + HH + d0 + DCOLS)
                        dj = slice(j * DCOLS, (j + 1) * DCOLS)
                        v.tensor_tensor(dfb[:], hlb[:, dj], hrb[:, dj], ALU.add)
                        v.scalar_tensor_tensor(dfb[:], hcc[:, sj], -2.0, dfb[:],
                                               ALU.mult, ALU.add)
                        v.scalar_tensor_tensor(
                            dfb[:], dfb[:], colsF[:, C_DCOEF + j:C_DCOEF + j + 1],
                            hcc[:, sj], ALU.mult, ALU.add)
                        xj = X1[:, j * T:(j + 1) * T][:, cc * TCA + d0:
                                                      cc * TCA + d0 + DCOLS]
                        last = v.tensor_tensor(xj, xj, dfb[:], ALU.add)
                    inc(last, "V")
                barrier()
        if upto == "p1dbg":
            @blk.sync
            def _(q):
                for nm in ("T", "S", "V", "GD", "SD"):
                    q.wait_ge(sems[nm], cnt[nm])
                q.dma_start(out=d_y[0:128, :], in_=hcc[:, 0:2048]).then_inc(sems["LD"], 16)
                q.dma_start(out=d_y[128:256, :], in_=asb[:, 0:2048]).then_inc(sems["LD"], 16)
                q.dma_start(out=d_y[256:384, :], in_=bsb[:, 0:2048]).then_inc(sems["LD"], 16)
                q.dma_start(out=d_y[384:512, :], in_=zb[:, 0:2048]).then_inc(sems["LD"], 16)
                q.dma_start(out=d_y[512:640, :], in_=htb[:, 0:2048]).then_inc(sems["LD"], 16)
                q.dma_start(out=d_y[768:896, :], in_=X1[:, 0:2048]).then_inc(sems["LD"], 16)
                cnt["LD"] += 16 * 6
                wait(q, "LD")

            @blk.gpsimd
            def _(g):
                wait(g, "LD")
                g.dma_start(out=d_y[640:768, :], in_=xnb[:, 0:2048]).then_inc(sems["GD"], 16)
                cnt["GD"] += 16
                wait(g, "GD")

    def tap_X1_and_finish():
        @blk.sync
        def _(q):
            for nm in ("T", "S", "V", "GD", "SD"):
                q.wait_ge(sems[nm], cnt[nm])
            for j in range(ND):
                q.dma_start(out=d_y[j * 128:(j + 1) * 128, :],
                            in_=X1[:, j * T:(j + 1) * T]).then_inc(sems["LD"], 16)
            cnt["LD"] += 16 * ND
            wait(q, "LD")

    def tap_xafter_and_finish():
        @blk.sync
        def _(q):
            for nm in ("T", "S", "V", "GD", "SD"):
                q.wait_ge(sems[nm], cnt[nm])
            q.dma_start(out=d_y, in_=d_xafter).then_inc(sems["LD"], 16)
            cnt["LD"] += 16
            wait(q, "LD")

    def ffn_chunks(sink_tensor, bufs):
        sink = "X1"
        X1s = sink_tensor
        h1b, lorab, lorab2, wfcbuf, wpjbuf = bufs
        for cc in range(T // TCF):
            cs = slice(cc * TCF, (cc + 1) * TCF)

            @blk.tensor
            def _(tE):
                wait(tE, "V")
                for k in range(ND):
                    tE.matmul(psL16[:, 0:TCF],
                              afcB[:, k * 16:(k + 1) * 16],
                              xnb[:, k * T:(k + 1) * T][:, cs],
                              start=(k == 0), stop=(k == ND - 1))
                inc(tdrain(tE), "T")

            @blk.scalar
            def _(s):
                wait(s, "T")
                last = s.copy(lorab[:], psL16[:, 0:TCF])
                inc(last, "S")

            items = []
            for m in range(32):
                def mk_load(m=m):
                    def f(q):
                        q.dma_start(out=wfcbuf[:, (m % 2) * 1024:(m % 2 + 1) * 1024],
                                    in_=d_wqfcB[m * 128:(m + 1) * 128, :]
                                    ).then_inc(sems["LD"], 16)
                        return 1
                    return f

                def mk_mm(m=m, cs=cs):
                    def f(tE, ps):
                        lastk = None
                        for k in range(ND):
                            lastk = tE.matmul(ps[:, 0:TCF],
                                      wfcbuf[:, (m % 2) * 1024 + k * 128:
                                             (m % 2) * 1024 + (k + 1) * 128],
                                      xnb[:, k * T:(k + 1) * T][:, cs],
                                      start=(k == 0),
                                      stop=(DBGNOLORA and k == ND - 1))
                        if DBGNOLORA:
                            return lastk
                        return tE.matmul(ps[:, 0:TCF],
                                         bfc2B[:, m * 128:(m + 1) * 128],
                                         lorab[:], start=False, stop=True)
                    return f

                def mk_ev(m=m):
                    def f(sE, ps):
                        return sE.activation(h1b[:, m * TCF:(m + 1) * TCF],
                                             ps[:, 0:TCF], AF.Gelu,
                                             bias=colsF[:, C_BFC + m:C_BFC + m + 1])
                    return f
                items.append({"mms": mk_mm(), "evac": mk_ev(), "load": mk_load()})
            mm_stage(items, [psA, psB])
            barrier()

            @blk.tensor
            def _(tE):
                for k in range(32):
                    tE.matmul(psL16[:, 0:TCF],
                              aprojB[:, k * 16:(k + 1) * 16],
                              h1b[:, k * TCF:(k + 1) * TCF],
                              start=(k == 0), stop=(k == 31))
                inc(tdrain(tE), "T")

            @blk.scalar
            def _(s):
                wait(s, "T")
                last = s.copy(lorab2[:], psL16[:, 0:TCF])
                inc(last, "S")

            items = []
            for m in range(ND):
                def mk_load(m=m):
                    def f(q):
                        q.dma_start(out=wpjbuf[:, (m % 2) * 4096:(m % 2 + 1) * 4096],
                                    in_=d_wqpjB[m * 128:(m + 1) * 128, :]
                                    ).then_inc(sems["LD"], 16)
                        return 1
                    return f

                def mk_mm(m=m):
                    def f(tE, ps):
                        lastk = None
                        for k in range(32):
                            lastk = tE.matmul(ps[:, 0:TCF],
                                      wpjbuf[:, (m % 2) * 4096 + k * 128:
                                             (m % 2) * 4096 + (k + 1) * 128],
                                      h1b[:, k * TCF:(k + 1) * TCF],
                                      start=(k == 0),
                                      stop=(DBGNOLORA and k == 31))
                        if DBGNOLORA:
                            return lastk
                        return tE.matmul(ps[:, 0:TCF],
                                         bproj2B[:, m * 128:(m + 1) * 128],
                                         lorab2[:], start=False, stop=True)
                    return f

                def mk_ev(m=m, cs=cs):
                    def f(vE, ps):
                        xj = X1s[:, m * T:(m + 1) * T][:, cs]
                        return vE.scalar_tensor_tensor(
                            xj, ps[:, 0:TCF],
                            colsF[:, C_BPJ + m:C_BPJ + m + 1], xj,
                            ALU.add, ALU.add)
                    return f
                items.append({"mms": mk_mm(), "evac": mk_ev(), "load": mk_load()})
            mm_stage(items, [psC, psD], evac_eng="V")
            barrier()

    if upto == "p1dbg":
        x1_stack.close()
        ctx.close()
        return nc

    if upto == "p1":
        tap_X1_and_finish()
        x1_stack.close()
        ctx.close()
        return nc

    with ExitStack() as p2:
        pr = p2.enter_context
        with ExitStack() as lnscope:
            lr = lnscope.enter_context
            scr = (lr(nc.sbuf_tensor("f_xb", [128, ND * 512], BF16, side="right")),
                   lr(nc.sbuf_tensor("f_sq", [128, ND * 512], BF16, side="right")),
                   lr(nc.sbuf_tensor("f_m", [128, 512], F32, side="right")),
                   lr(nc.sbuf_tensor("f_r", [128, 512], F32, side="right")),
                   lr(nc.sbuf_tensor("f_t", [128, 512], F32, side="right")))
            emit_ln(lambda j, cs: X1[:, j * T:(j + 1) * T][:, cs], scr)
            barrier()
        bufs = (pr(nc.sbuf_tensor("f_h1", [128, 32 * TCF], BF16, side="right")),
                pr(nc.sbuf_tensor("f_lr", [16, TCF], BF16, side="right")),
                pr(nc.sbuf_tensor("f_lr2", [16, TCF], BF16, side="right")),
                pr(nc.sbuf_tensor("f_wfc", [128, 2 * 1024], BF16, side="right")),
                pr(nc.sbuf_tensor("f_wpj", [128, 2 * 4096], BF16, side="right")))
        ffn_chunks(X1, bufs)

    if upto == "p2":
        tap_X1_and_finish()
        x1_stack.close()
        ctx.close()
        return nc

    with ExitStack() as p3ln:
        lr = p3ln.enter_context
        scr = (lr(nc.sbuf_tensor("m_xb", [128, ND * 512], BF16, side="right")),
               lr(nc.sbuf_tensor("m_sq", [128, ND * 512], BF16, side="right")),
               lr(nc.sbuf_tensor("m_m", [128, 512], F32, side="right")),
               lr(nc.sbuf_tensor("m_r", [128, 512], F32, side="right")),
               lr(nc.sbuf_tensor("m_t", [128, 512], F32, side="right")))
        emit_ln(lambda j, cs: X1[:, j * T:(j + 1) * T][:, cs], scr)
        barrier()

        @blk.sync
        def _(q):
            for j in range(ND):
                q.dma_start(out=d_xspill[j * 128:(j + 1) * 128, :],
                            in_=X1[:, j * T:(j + 1) * T]).then_inc(sems["LD"], 16)
            cnt["LD"] += 16 * ND
        barrier()
    x1_stack.close()

    with ExitStack() as p3:
        pr = p3.enter_context
        xib = pr(nc.sbuf_tensor("xib", [128, NDI * (TC + 8)], BF16, side="right"))
        gb = pr(nc.sbuf_tensor("gb", [128, NDI * TC], BF16, side="right"))
        xcb = pr(nc.sbuf_tensor("xcb", [128, NDI * TC], BF16, side="right"))
        brep = pr(nc.sbuf_tensor("brep", [128, NS * TC], BF16, side="right"))
        crep = pr(nc.sbuf_tensor("crep", [128, NS * TC], BF16, side="right"))
        HH = 4
        TCH = TC + HH
        hgb = pr(nc.sbuf_tensor("hgb", [128, SG * TCH], F32, side="right"))
        hcg = pr(nc.sbuf_tensor("hcg", [128, SG * TC], F32, side="right"))
        treb = pr(nc.sbuf_tensor("treb", [128, 2 * TC], F32, side="right"))
        delb = pr(nc.sbuf_tensor("delb", [128, 2 * TC], F32, side="right"))
        dub = pr(nc.sbuf_tensor("dub", [128, TC], F32, side="right"))
        dab = pr(nc.sbuf_tensor("dab", [128, 2 * TCH], F32, side="right"))
        dbub = pr(nc.sbuf_tensor("dbub", [128, NS * TCH], F32, side="right"))
        yb = pr(nc.sbuf_tensor("yb", [128, TC], F32, side="right"))
        ydb = pr(nc.sbuf_tensor("ydb", [128, TC], F32, side="right"))
        dtbuf = pr(nc.sbuf_tensor("dtbuf", [64, TC], BF16, side="right"))
        bcb = pr(nc.sbuf_tensor("bcb", [16, 2 * TC], BF16, side="right"))
        spt = pr(nc.sbuf_tensor("spt", [128, 4 * TC], F32, side="right"))
        inwbuf = pr(nc.sbuf_tensor("inwbuf", [128, 2 * 1024], BF16, side="right"))
        opwbuf = pr(nc.sbuf_tensor("opwbuf", [128, 2 * 2048], BF16, side="right"))
        xpwb = pr(nc.sbuf_tensor("xpwb", [128, NDI * 96], BF16, side="right"))
        dtwb = pr(nc.sbuf_tensor("dtwb", [64, DI], BF16, side="right"))
        xstr = pr(nc.sbuf_tensor("xstr", [128, ND * TC], F32, side="right"))
        otb = pr(nc.sbuf_tensor("otb", [128, ND * TC], F32, side="right"))

        nlayer_run = _NL[0] if upto == "p3dbg" else NLAYER
        for layer in range(nlayer_run):
            is_last = layer == nlayer_run - 1

            @blk.sync
            def _(q):
                q.dma_start(out=cwF[:], in_=d_cw[layer]).then_inc(sems["LD"], 16)
                q.dma_start(out=colsF[:, C_CB:C_CB + 16],
                            in_=d_cb[layer]).then_inc(sems["LD"], 16)
                q.dma_start(out=colsF[:, C_DTB:C_DTB + 16],
                            in_=d_dtb[layer]).then_inc(sems["LD"], 16)
                q.dma_start(out=colsF[:, C_DSK:C_DSK + 16],
                            in_=d_dsk[layer]).then_inc(sems["LD"], 16)
                q.dma_start(out=acolF[:], in_=d_alog[layer]).then_inc(sems["LD"], 16)
                for k in range(NDI):
                    q.dma_start(out=xpwb[:, k * 96:(k + 1) * 96],
                                in_=d_xpwB[layer][k * 128:(k + 1) * 128, :]
                                ).then_inc(sems["LD"], 16)
                q.dma_start(out=dtwb[:], in_=d_dtwB[layer]).then_inc(sems["LD"], 16)
                cnt["LD"] += 16 * (6 + NDI)

            @blk.scalar
            def _(s):
                wait(s, "LD")
                last = s.activation(acolF[:], acolF[:], AF.Exp)
                inc(last, "S")

            @blk.vector
            def _(v):
                wait(v, "S")
                v.tensor_scalar(acolF[:], acolF[:], -1.0, None, ALU.mult)
                v.memset(xib[:], 0.0)
                v.memset(dbub[:], 0.0)
                last = v.memset(dab[:], 0.0)
                inc(last, "V")
            barrier()

            for cc in range(NCH):
                cs = slice(cc * TC, (cc + 1) * TC)

                if cc > 0:
                    @blk.vector
                    def _(v):
                        wait(v, "S")
                        for j in range(NDI):
                            b0 = j * (TC + 8)
                            dsth = xib[:, b0:b0 + 8].bitcast(mybir.dt.uint32)
                            srch = xib[:, b0 + TC:b0 + TC + 8].bitcast(
                                mybir.dt.uint32)
                            last = v.tensor_tensor(dsth, srch, srch,
                                                   ALU.bitwise_or)
                        inc(last, "V")

                    @blk.scalar
                    def _(s):
                        wait(s, "V")
                barrier()

                items = []
                for m in range(32):
                    def mk_load(layer=layer, m=m):
                        def f(q):
                            q.dma_start(out=inwbuf[:, (m % 2) * 1024:(m % 2 + 1) * 1024],
                                        in_=d_inwB[layer][m * 128:(m + 1) * 128, :]
                                        ).then_inc(sems["LD"], 16)
                            return 1
                        return f

                    def mk_mm(m=m, cs=cs):
                        def f(tE, ps):
                            for k in range(ND):
                                last = tE.matmul(
                                    ps[:, 0:TC],
                                    inwbuf[:, (m % 2) * 1024 + k * 128:
                                           (m % 2) * 1024 + (k + 1) * 128],
                                    xnb[:, k * T:(k + 1) * T][:, cs],
                                    start=(k == 0), stop=(k == ND - 1))
                            return last
                        return f

                    if m < 16:
                        def mk_ev(m=m):
                            def f(sE, ps):
                                b0 = m * (TC + 8)
                                return sE.copy(xib[:, b0 + 8:b0 + 8 + TC],
                                               ps[:, 0:TC])
                            return f
                    else:
                        def mk_ev(m=m):
                            def f(sE, ps):
                                return sE.activation(
                                    gb[:, (m - 16) * TC:(m - 15) * TC],
                                    ps[:, 0:TC], AF.Silu)
                            return f
                    items.append({"mms": mk_mm(), "evac": mk_ev(),
                                  "load": mk_load()})
                mm_stage(items, [psA, psB])
                barrier()

                v0c2 = cnt["V"]
                s0c2 = cnt["S"]

                @blk.vector
                def _(v):
                    for j in range(NDI):
                        b0 = j * (TC + 8)
                        if j >= 2:
                            v.wait_ge(sems["S"], s0c2 + j - 1)
                        sl = spt[:, (j % 2) * TC:(j % 2 + 1) * TC]
                        v.tensor_scalar(sl, xib[:, b0 + 5:b0 + 5 + TC],
                                        cwF[:, j * 4:j * 4 + 1], None, ALU.mult)
                        for tp in range(1, 4):
                            last = v.scalar_tensor_tensor(
                                sl, xib[:, b0 + 5 + tp:b0 + 5 + tp + TC],
                                cwF[:, j * 4 + tp:j * 4 + tp + 1], sl,
                                ALU.mult, ALU.add)
                        last.then_inc(sems["V"], 1)
                        cnt["V"] += 1

                @blk.scalar
                def _(s):
                    for j in range(NDI):
                        s.wait_ge(sems["V"], v0c2 + j + 1)
                        last = s.activation(xcb[:, j * TC:(j + 1) * TC],
                                            spt[:, (j % 2) * TC:(j % 2 + 1) * TC],
                                            AF.Silu,
                                            bias=colsF[:, C_CB + j:C_CB + j + 1])
                        last.then_inc(sems["S"], 1)
                        cnt["S"] += 1
                barrier()

                @blk.tensor
                def _(tE):
                    for k in range(NDI):
                        tE.matmul(psS96[0:64, 0:TC],
                                  xpwb[:, k * 96:k * 96 + 64],
                                  xcb[:, k * TC:(k + 1) * TC],
                                  start=(k == 0), stop=(k == NDI - 1))
                    for k in range(NDI):
                        tE.matmul(psL16[:, 0:TC],
                                  xpwb[:, k * 96 + 64:k * 96 + 80],
                                  xcb[:, k * TC:(k + 1) * TC],
                                  start=(k == 0), stop=(k == NDI - 1))
                    for k in range(NDI):
                        tE.matmul(psL16b[:, 0:TC],
                                  xpwb[:, k * 96 + 80:k * 96 + 96],
                                  xcb[:, k * TC:(k + 1) * TC],
                                  start=(k == 0), stop=(k == NDI - 1))
                    inc(tdrain(tE), "T")

                @blk.scalar
                def _(s):
                    wait(s, "T")
                    s.copy(dtbuf[:], psS96[0:64, 0:TC])
                    s.copy(bcb[:, 0:TC], psL16[:, 0:TC])
                    last = s.copy(bcb[:, TC:2 * TC], psL16b[:, 0:TC])
                    inc(last, "S")
                barrier()

                items = []
                for which in range(2):
                    dstb = brep if which == 0 else crep
                    for s_ in range(NS):
                        def mk_mm(s_=s_, which=which):
                            def f(tE, ps):
                                return tE.matmul(ps[:, 0:TC],
                                                 selB[:, s_ * 128:(s_ + 1) * 128],
                                                 bcb[:, which * TC:(which + 1) * TC],
                                                 start=True, stop=True)
                            return f

                        def mk_ev(s_=s_, dstb=dstb):
                            def f(sE, ps):
                                return sE.copy(dstb[:, s_ * TC:(s_ + 1) * TC],
                                               ps[:, 0:TC])
                            return f
                        items.append({"mms": mk_mm(), "evac": mk_ev()})
                mm_stage(items, [psC, psD])
                barrier()

                vprot = {}
                for j in range(NDI):
                    pj = j % 2

                    @blk.tensor
                    def _(tE):
                        if j >= 2:
                            wait(tE, "V", vprot[j - 2])
                        tE.matmul((psA if pj == 0 else psB)[:, 0:TC],
                                  dtwb[:, j * 128:(j + 1) * 128],
                                  dtbuf[:], start=True, stop=True)
                        inc(tdrain(tE), "T")

                    @blk.scalar
                    def _(s):
                        wait(s, "T")
                        if j >= 1:
                            wait(s, "V", vprot.get(j - 1, 0))
                        ps = (psA if pj == 0 else psB)
                        sp0 = spt[:, 2 * TC:3 * TC]
                        sp1 = spt[:, 3 * TC:4 * TC]
                        s.activation(sp0, ps[:, 0:TC], AF.Abs,
                                     bias=colsF[:, C_DTB + j:C_DTB + j + 1])
                        s.activation(sp1, sp0, AF.Exp,
                                     scale=colsF[:, C_NEG1:C_NEG1 + 1])
                        s.activation(sp0, sp1, AF.Ln,
                                     bias=colsF[:, C_ONE:C_ONE + 1])
                        last = s.activation(sp1, ps[:, 0:TC], AF.Relu,
                                            bias=colsF[:, C_DTB + j:C_DTB + j + 1])
                        inc(last, "S")

                    @blk.vector
                    def _(v):
                        wait(v, "S")
                        dj = delb[:, pj * TC:(pj + 1) * TC]
                        v.tensor_tensor(dj, spt[:, 2 * TC:3 * TC],
                                        spt[:, 3 * TC:4 * TC], ALU.add)
                        last = v.tensor_tensor(dub[:], dj,
                                               xcb[:, j * TC:(j + 1) * TC],
                                               ALU.mult)
                        inc(last, "V")
                    vdel = cnt["V"]

                    s1 = cnt["S"]
                    v1 = cnt["V"]

                    @blk.scalar
                    def _(s):
                        for si in range(NS):
                            if si < 2:
                                s.wait_ge(sems["V"], v1)
                            else:
                                s.wait_ge(sems["V"], v1 + si)
                            last = s.activation(
                                dab[:, (si % 2) * TCH + HH:(si % 2 + 1) * TCH],
                                delb[:, pj * TC:(pj + 1) * TC], AF.Exp,
                                scale=acolF[:, j * NS + si:j * NS + si + 1])
                            last.then_inc(sems["S"], 1)
                            cnt["S"] += 1

                    @blk.vector
                    def _(v):
                        dbu_out = dbub[:].rearrange(
                            "p (s t) -> p s t", s=NS)[:, :, HH:TCH]
                        dub_b = dub[:].unsqueeze(1).broadcast_to([128, NS, TC])
                        brep_v = brep[:].rearrange("p (s t) -> p s t", s=NS)
                        last = v.tensor_tensor(dbu_out, dub_b, brep_v, ALU.mult)
                        last.then_inc(sems["V"], 1)
                        cnt["V"] += 1
                        v.wait_ge(sems["V"], cnt["V"])
                        for si in range(NS):
                            v.wait_ge(sems["S"], s1 + si + 1)
                            sseg = si * TCH
                            gslot = si % SG
                            gseg = gslot * TCH
                            last = v.tensor_tensor_scan(
                                hgb[:, gseg:gseg + TCH],
                                dab[:, (si % 2) * TCH:(si % 2 + 1) * TCH],
                                dbub[:, sseg:sseg + TCH], 0.0,
                                ALU.mult, ALU.add)
                            last.then_inc(sems["V"], 1)
                            cnt["V"] += 1
                            if gslot == SG - 1:
                                v.wait_ge(sems["V"], cnt["V"])
                                g0 = si - SG + 1
                                hsl = hgb[:].rearrange(
                                    "p (s t) -> p s t", s=SG)[:, :, TCH - HH:TCH]
                                dsl = dbub[:].rearrange(
                                    "p (s t) -> p s t", s=NS)[:, g0:g0 + SG, 0:HH]
                                v.tensor_tensor(dsl, hsl, hsl, ALU.max)
                                hgd = hgb[:].rearrange(
                                    "p (s t) -> p s t", s=SG)[:, :, HH:TCH]
                                v.tensor_tensor(
                                    hcg[:].rearrange("p (s t) -> p s t", s=SG),
                                    hgd,
                                    crep[:, g0 * TC:(g0 + SG) * TC].rearrange(
                                        "p (s t) -> p s t", s=SG),
                                    ALU.mult)
                                v.tensor_tensor(treb[:], hcg[:, 0:2 * TC],
                                                hcg[:, 2 * TC:4 * TC], ALU.add)
                                if g0 == 0:
                                    v.tensor_tensor(yb[:], treb[:, 0:TC],
                                                    treb[:, TC:2 * TC], ALU.add)
                                else:
                                    v.tensor_tensor(treb[:, 0:TC], treb[:, 0:TC],
                                                    treb[:, TC:2 * TC], ALU.add)
                                    v.tensor_tensor(yb[:], yb[:],
                                                    treb[:, 0:TC], ALU.add)
                        v.scalar_tensor_tensor(ydb[:], xcb[:, j * TC:(j + 1) * TC],
                                               colsF[:, C_DSK + j:C_DSK + j + 1],
                                               yb[:], ALU.mult, ALU.add)
                        last = v.tensor_tensor(xcb[:, j * TC:(j + 1) * TC],
                                               ydb[:], gb[:, j * TC:(j + 1) * TC],
                                               ALU.mult)
                        last.then_inc(sems["V"], 1)
                        cnt["V"] += 1
                    vprot[j] = cnt["V"]
                barrier()

                ldx0 = cnt["LD"]
                if is_last:
                    @blk.sync
                    def _(q):
                        wait(q, "GD")
                        for m in range(ND):
                            q.dma_start(out=xstr[:, m * TC:(m + 1) * TC],
                                        in_=d_xspill[m * 128:(m + 1) * 128, :][:, cs]
                                        ).then_inc(sems["LD"], 16)
                        cnt["LD"] += 16 * ND

                items = []
                for m in range(ND):
                    def mk_load(layer=layer, m=m):
                        def f(q):
                            q.dma_start(out=opwbuf[:, (m % 2) * 2048:(m % 2 + 1) * 2048],
                                        in_=d_opwB[layer][m * 128:(m + 1) * 128, :]
                                        ).then_inc(sems["LD"], 16)
                            return 1
                        return f

                    def mk_mm(m=m):
                        def f(tE, ps):
                            for k in range(NDI):
                                last = tE.matmul(
                                    ps[:, 0:TC],
                                    opwbuf[:, (m % 2) * 2048 + k * 128:
                                           (m % 2) * 2048 + (k + 1) * 128],
                                    xcb[:, k * TC:(k + 1) * TC],
                                    start=(k == 0), stop=(k == NDI - 1))
                            return last
                        return f

                    if not is_last:
                        def mk_ev(m=m, cs=cs):
                            def f(sE, ps):
                                return sE.copy(xnb[:, m * T:(m + 1) * T][:, cs],
                                               ps[:, 0:TC])
                            return f
                    else:
                        def mk_ev(m=m, ldx0=ldx0):
                            def f(sE, ps):
                                return sE.copy(otb[:, m * TC:(m + 1) * TC],
                                               ps[:, 0:TC])
                            return f
                    items.append({"mms": mk_mm(), "evac": mk_ev(),
                                  "load": mk_load()})
                mm_stage(items, [psC, psD])

                if is_last:
                    s0c6 = cnt["S"] - ND
                    ldx1 = ldx0 + 16 * ND

                    @blk.vector
                    def _(v):
                        v.wait_ge(sems["LD"], ldx1)
                        for m in range(ND):
                            v.wait_ge(sems["S"], s0c6 + m + 1)
                            last = v.tensor_tensor(xstr[:, m * TC:(m + 1) * TC],
                                                   xstr[:, m * TC:(m + 1) * TC],
                                                   otb[:, m * TC:(m + 1) * TC],
                                                   ALU.add)
                            last.then_inc(sems["V"], 1)
                            cnt["V"] += 1

                    vv0 = cnt["V"] - ND

                    @blk.gpsimd
                    def _(g):
                        for m in range(ND):
                            g.wait_ge(sems["V"], vv0 + m + 1)
                            g.dma_start(out=d_xafter[m * 128:(m + 1) * 128, :][:, cs],
                                        in_=xstr[:, m * TC:(m + 1) * TC]
                                        ).then_inc(sems["GD"], 16)
                        cnt["GD"] += 16 * ND
                barrier()

        if upto == "p3dbg":
            @blk.sync
            def _(q):
                for nm in ("T", "S", "V", "GD", "SD"):
                    q.wait_ge(sems[nm], cnt[nm])
                q.dma_start(out=d_y[512:640, 0:256], in_=acolF[:]).then_inc(sems["LD"], 16)
                q.dma_start(out=d_y[768:896, 0:1040], in_=hgb[:]).then_inc(sems["LD"], 16)
                q.dma_start(out=d_y[640:768, 0:512], in_=delb[:]).then_inc(sems["LD"], 16)
                cnt["LD"] += 16 * 3
                wait(q, "LD")

            @blk.gpsimd
            def _(g):
                wait(g, "LD")
                g.dma_start(out=d_y[0:128, :], in_=gb[:, 0:2048]).then_inc(sems["GD"], 16)
                g.dma_start(out=d_y[128:256, :], in_=xcb[:, 0:2048]).then_inc(sems["GD"], 16)
                g.dma_start(out=d_y[256:384, :], in_=xib[:, 0:2048]).then_inc(sems["GD"], 16)
                g.dma_start(out=d_y[448:464, 0:512], in_=bcb[:]).then_inc(sems["GD"], 16)
                g.dma_start(out=d_y[384:448, 0:256], in_=dtbuf[:]).then_inc(sems["GD"], 16)
                g.dma_start(out=d_y[896:1024, :], in_=xnb[:, 0:2048]).then_inc(sems["GD"], 16)
                cnt["GD"] += 16 * 6
                wait(g, "GD")

    if upto == "p3dbg":
        ctx.close()
        return nc

    if upto == "p3":
        tap_xafter_and_finish()
        ctx.close()
        return nc

    with ExitStack() as p4:
        pr = p4.enter_context
        X4 = pr(nc.sbuf_tensor("X4", [128, ND * T], F32, side="right"))

        @blk.sync
        def _(q):
            wait(q, "GD")
            for j in range(ND):
                q.dma_start(out=X4[:, j * T:(j + 1) * T],
                            in_=d_xafter[j * 128:(j + 1) * 128, :]
                            ).then_inc(sems["LD"], 16)
            cnt["LD"] += 16 * ND
        barrier()

        with ExitStack() as lnscope:
            lr = lnscope.enter_context
            scr = (lr(nc.sbuf_tensor("g_xb", [128, ND * 512], BF16, side="right")),
                   lr(nc.sbuf_tensor("g_sq", [128, ND * 512], BF16, side="right")),
                   lr(nc.sbuf_tensor("g_m", [128, 512], F32, side="right")),
                   lr(nc.sbuf_tensor("g_r", [128, 512], F32, side="right")),
                   lr(nc.sbuf_tensor("g_t", [128, 512], F32, side="right")))
            emit_ln(lambda j, cs: X4[:, j * T:(j + 1) * T][:, cs], scr)
            barrier()

        bufs = (pr(nc.sbuf_tensor("g_h1", [128, 32 * TCF], BF16, side="right")),
                pr(nc.sbuf_tensor("g_lr", [16, TCF], BF16, side="right")),
                pr(nc.sbuf_tensor("g_lr2", [16, TCF], BF16, side="right")),
                pr(nc.sbuf_tensor("g_wfc", [128, 2 * 1024], BF16, side="right")),
                pr(nc.sbuf_tensor("g_wpj", [128, 2 * 4096], BF16, side="right")))
        ffn_chunks(X4, bufs)

        @blk.gpsimd
        def _(g):
            for nm in ("T", "S", "V", "LD", "SD"):
                g.wait_ge(sems[nm], cnt[nm])
            for j in range(ND):
                g.dma_start(out=d_y[j * 128:(j + 1) * 128, :],
                            in_=X4[:, j * T:(j + 1) * T]).then_inc(sems["GD"], 16)
            cnt["GD"] += 16 * ND
            wait(g, "GD")

    ctx.close()
    return nc


_CACHE = {}
_UPTO = ["full"]


def kernel(**inputs):
    x = np.asarray(inputs["L0_attn_a_x"])
    B = x.shape[0]

    def col(v, n):
        return np.ascontiguousarray(np.asarray(v).reshape(n, 128).T)

    shared = {
        "wz": np.ascontiguousarray(inputs["L0_attn_a_Wz"]),
        "wh": np.ascontiguousarray(inputs["L0_attn_a_Wh"]),
        "bz": col(inputs["L0_attn_a_bz"], ND),
        "bh": col(inputs["L0_attn_a_bh"], ND),
        "logd": col(inputs["L0_attn_a_log_D"], ND),
        "wfcT": np.ascontiguousarray(np.asarray(inputs["L0_ffn_W_fc"]).T),
        "wprojT": np.ascontiguousarray(np.asarray(inputs["L0_ffn_W_proj"]).T),
        "afcT": np.ascontiguousarray(np.asarray(inputs["L0_ffn_A_fc"]).T),
        "bfcT": np.ascontiguousarray(np.asarray(inputs["L0_ffn_B_fc"]).T),
        "aprojT": np.ascontiguousarray(np.asarray(inputs["L0_ffn_A_proj"]).T),
        "bprojT": np.ascontiguousarray(np.asarray(inputs["L0_ffn_B_proj"]).T),
        "bfc": col(inputs["L0_ffn_b_fc"], 32),
        "bproj": col(inputs["L0_ffn_b_proj"], ND),
    }
    for l in range(NLAYER):
        shared[f"inwT{l}"] = np.ascontiguousarray(
            np.asarray(inputs["L0_attn_b_in_proj_w"])[l].T)
        shared[f"xpwT{l}"] = np.ascontiguousarray(
            np.asarray(inputs["L0_attn_b_x_proj_w"])[l].T)
        shared[f"dtwT{l}"] = np.ascontiguousarray(
            np.asarray(inputs["L0_attn_b_dt_proj_w"])[l].T)
        shared[f"opwT{l}"] = np.ascontiguousarray(
            np.asarray(inputs["L0_attn_b_out_proj_w"])[l].T)
        cw = np.asarray(inputs["L0_attn_b_conv_w"])[l]
        shared[f"cw{l}"] = np.ascontiguousarray(
            cw.reshape(NDI, 128, 4).transpose(1, 0, 2).reshape(128, NDI * 4))
        shared[f"cb{l}"] = col(np.asarray(inputs["L0_attn_b_conv_b"])[l], NDI)
        shared[f"dtb{l}"] = col(np.asarray(inputs["L0_attn_b_dt_proj_b"])[l], NDI)
        al = np.asarray(inputs["L0_attn_b_A_log"])[l]
        shared[f"alog{l}"] = np.ascontiguousarray(
            al.reshape(NDI, 128, NS).transpose(1, 0, 2).reshape(128, NDI * NS))
        shared[f"dsk{l}"] = col(np.asarray(inputs["L0_attn_b_D"])[l], NDI)

    key = _UPTO[0]
    if key not in _CACHE:
        _CACHE[key] = build(key)
    nc = _CACHE[key]

    in_maps = []
    for i in range(B):
        m = {"x": np.ascontiguousarray(x[i].T)}
        m.update(shared)
        in_maps.append(m)
    res = run_bass_kernel_spmd(nc, in_maps, core_ids=list(range(B)))
    out = np.stack([np.asarray(r["y"]).T for r in res.results], axis=0)
    return out.astype(np.float32)


# version 12
# speedup vs baseline: 1.0046x; 1.0046x over previous
from contextlib import ExitStack

import numpy as np

import concourse.bass as bass
import concourse.mybir as mybir
from concourse.bass_isa import ReduceOp
from concourse.bass_utils import run_bass_kernel_spmd

F32 = mybir.dt.float32
BF16 = mybir.dt.bfloat16
AF = mybir.ActivationFunctionType
ALU = mybir.AluOpType
AX = mybir.AxisListType

D = 1024
T = 2048
DI = 2048
NS = 16
ND = D // 128
NDI = DI // 128
FF = 4096
TC = 256
NCH = T // TC
TCA = 256
TCF = 512
DCOLS = 256
SG = 4
MAGIC = 12582912.0
NLAYER = 4
_NL = [4]
DBGNOCARRY = False
DBGNOLORA = False
DBGNOQUANT = False
DBGQUANTP1ONLY = False
DBGQUANTSTAGE = 0

C_BZ = 0
C_BH = 8
C_DCOEF = 24
C_BFC = 32
C_BPJ = 64
C_EPS = 72
C_ONE = 73
C_IFC = 128
C_SFC = 136
C_IPJ = 160
C_SPJ = 168
C_127 = 184
C_R127 = 185
C_INVD = 79
C_CB = 80
C_DTB = 96
C_DSK = 112
C_NEG1 = 176
C_LOGD = 144


def build(upto="full"):
    nc = bass.Bass("TRN2", target_bir_lowering=False, debug=False)

    def di(name, shape, dt=F32):
        return nc.dram_tensor(name, shape, dt, kind="ExternalInput").ap()

    def dint(name, shape, dt):
        return nc.dram_tensor(name, shape, dt, kind="Internal").ap()

    d_x = di("x", [D, T])
    d_wz = di("wz", [D, D])
    d_wh = di("wh", [D, D])
    d_bz = di("bz", [128, ND])
    d_bh = di("bh", [128, ND])
    d_logd = di("logd", [128, ND])
    d_wfcT = di("wfcT", [D, FF])
    d_wprojT = di("wprojT", [FF, D])
    d_afcT = di("afcT", [D, 16])
    d_bfcT = di("bfcT", [16, FF])
    d_aprojT = di("aprojT", [FF, 16])
    d_bprojT = di("bprojT", [16, D])
    d_bfc = di("bfc", [128, 32])
    d_bproj = di("bproj", [128, ND])
    d_inwT = [di(f"inwT{l}", [D, 2 * DI]) for l in range(NLAYER)]
    d_xpwT = [di(f"xpwT{l}", [DI, 96]) for l in range(NLAYER)]
    d_dtwT = [di(f"dtwT{l}", [64, DI]) for l in range(NLAYER)]
    d_opwT = [di(f"opwT{l}", [DI, D]) for l in range(NLAYER)]
    d_cw = [di(f"cw{l}", [128, NDI * 4]) for l in range(NLAYER)]
    d_cb = [di(f"cb{l}", [128, NDI]) for l in range(NLAYER)]
    d_dtb = [di(f"dtb{l}", [128, NDI]) for l in range(NLAYER)]
    d_alog = [di(f"alog{l}", [128, NDI * NS]) for l in range(NLAYER)]
    d_dsk = [di(f"dsk{l}", [128, NDI]) for l in range(NLAYER)]

    d_y = nc.dram_tensor("y", [D, T], F32, kind="ExternalOutput").ap()

    d_xspill = dint("xspill", [D, T], F32)
    d_xafter = dint("xafter", [D, T], F32)
    d_wqfcB = dint("wqfcB", [32 * 128, 8 * 128], BF16)
    d_wqpjB = dint("wqpjB", [8 * 128, 32 * 128], BF16)
    d_inwB = [dint(f"inwB{l}", [32 * 128, 8 * 128], BF16) for l in range(NLAYER)]
    d_opwB = [dint(f"opwB{l}", [8 * 128, 16 * 128], BF16) for l in range(NLAYER)]
    d_xpwB = [dint(f"xpwB{l}", [DI, 96], BF16) for l in range(NLAYER)]
    d_dtwB = [dint(f"dtwB{l}", [64, DI], BF16) for l in range(NLAYER)]

    ctx = ExitStack()
    ec = ctx.enter_context

    sems = {}
    for nm in ("LD", "GD", "SD", "T", "S", "V"):
        sems[nm] = ec(nc.semaphore(nm))
    cnt = {nm: 0 for nm in sems}

    def inc(instr, nm, n=1):
        instr.then_inc(sems[nm], n)
        cnt[nm] += n
        return cnt[nm]

    def wait(eng, nm, val=None):
        eng.wait_ge(sems[nm], cnt[nm] if val is None else val)

    xnb = ec(nc.sbuf_tensor("xnb", [128, ND * T], BF16))
    colsF = ec(nc.sbuf_tensor("colsF", [128, 192], F32))
    acolF = ec(nc.sbuf_tensor("acolF", [128, NDI * NS], F32))
    cwF = ec(nc.sbuf_tensor("cwF", [128, NDI * 4], F32))
    selB = ec(nc.sbuf_tensor("selB", [16, 16 * 128], BF16))
    onesB = ec(nc.sbuf_tensor("onesB", [128, 128], BF16))
    afcB = ec(nc.sbuf_tensor("afcB", [128, ND * 16], BF16))
    bfc2B = ec(nc.sbuf_tensor("bfc2B", [16, FF], BF16))
    aprojB = ec(nc.sbuf_tensor("aprojB", [128, 32 * 16], BF16))
    bproj2B = ec(nc.sbuf_tensor("bproj2B", [16, D], BF16))
    mcol = ec(nc.sbuf_tensor("mcol", [128, 48], F32))
    qsc = ec(nc.sbuf_tensor("qsc", [128, 512], F32))

    x1_stack = ExitStack()
    X1 = x1_stack.enter_context(nc.sbuf_tensor("X1", [128, ND * T], F32))

    psA = ec(nc.psum_tensor("psA", [128, 512], F32))
    psB = ec(nc.psum_tensor("psB", [128, 512], F32))
    psC = ec(nc.psum_tensor("psC", [128, 512], F32))
    psD = ec(nc.psum_tensor("psD", [128, 512], F32))
    psS96 = ec(nc.psum_tensor("psS96", [96, 512], F32))
    psL16 = ec(nc.psum_tensor("psL16", [16, 512], F32))
    psL16b = ec(nc.psum_tensor("psL16b", [16, 512], F32))
    psX = ec(nc.psum_tensor("psX", [64, 64], F32))

    blk = ec(nc.Block())

    def tdrain(tE):
        return tE.matmul(psX[:, 0:64], onesB[:, 0:64], onesB[:, 64:128],
                         start=True, stop=True)

    def barrier():
        tgt = dict(cnt)
        for dec, own in ((blk.tensor, "T"), (blk.scalar, "S"), (blk.vector, "V"),
                         (blk.sync, None), (blk.gpsimd, None)):
            def mk(own=own):
                def f(e):
                    for nm in ("T", "S", "V", "LD", "GD", "SD"):
                        if nm != own:
                            e.wait_ge(sems[nm], tgt[nm])
                return f
            dec(mk())

    def mm_stage(items, banks, evac_eng="S"):
        nb = len(banks)
        t0 = cnt["T"]
        e0 = cnt[evac_eng]
        ld0 = cnt["LD"]
        nload = [0] * len(items)
        if any(it.get("load") for it in items):
            @blk.sync
            def _(q):
                for i, it in enumerate(items):
                    if not it.get("load"):
                        continue
                    if i >= 2:
                        q.wait_ge(sems["T"], t0 + i - 1)
                    nload[i] = it["load"](q)
                    cnt["LD"] += 16 * nload[i]
        cum = ld0
        cumload = []
        for n in nload:
            cum += 16 * n
            cumload.append(cum)

        sS0 = cnt["S"]
        sV0 = cnt["V"]

        @blk.tensor
        def _(tE):
            tE.wait_ge(sems["S"], sS0)
            tE.wait_ge(sems["V"], sV0)
            for i, it in enumerate(items):
                if nload[i]:
                    tE.wait_ge(sems["LD"], cumload[i])
                if i >= nb:
                    tE.wait_ge(sems[evac_eng], e0 + i - nb + 1)
                last = it["mms"](tE, banks[i % nb])
                last.then_inc(sems["T"], 1)
                cnt["T"] += 1
            last = tdrain(tE)
            last.then_inc(sems["T"], 1)
            cnt["T"] += 1

        dec = blk.scalar if evac_eng == "S" else blk.vector

        @dec
        def _(eE):
            for i, it in enumerate(items):
                eE.wait_ge(sems["T"], t0 + i + 2)
                last = it["evac"](eE, banks[i % nb])
                last.then_inc(sems[evac_eng], 1)
                cnt[evac_eng] += 1

    def emit_ln(src_ap, scr):
        xb, sqb, mb, rb, tb = scr
        for c0 in range(0, T, 512):
            cs = slice(c0, c0 + 512)

            @blk.scalar
            def _(s):
                wait(s, "V")
                wait(s, "T")
                for j in range(ND):
                    s.copy(xb[:, j * 512:(j + 1) * 512], src_ap(j, cs))
                for j in range(ND):
                    last = s.activation(sqb[:, j * 512:(j + 1) * 512],
                                        xb[:, j * 512:(j + 1) * 512], AF.Square)
                inc(last, "S")

            @blk.tensor
            def _(tE):
                wait(tE, "S")
                for j in range(ND):
                    tE.matmul(psA[:], onesB[:], xb[:, j * 512:(j + 1) * 512],
                              start=(j == 0), stop=(j == ND - 1))
                for j in range(ND):
                    tE.matmul(psB[:], onesB[:], sqb[:, j * 512:(j + 1) * 512],
                              start=(j == 0), stop=(j == ND - 1))
                inc(tdrain(tE), "T")

            @blk.scalar
            def _(s):
                wait(s, "T")
                last = s.activation(mb[:], psA[:], AF.Copy,
                                    scale=colsF[:, C_INVD:C_INVD + 1])
                inc(last, "S")

            @blk.vector
            def _(v):
                wait(v, "S")
                v.tensor_tensor(tb[:], mb[:], mb[:], ALU.mult)
                last = v.scalar_tensor_tensor(tb[:], psB[:],
                                              colsF[:, C_INVD:C_INVD + 1], tb[:],
                                              ALU.mult, ALU.subtract)
                inc(last, "V")

            @blk.scalar
            def _(s):
                wait(s, "V")
                last = s.activation(rb[:], tb[:], AF.Sqrt,
                                    bias=colsF[:, C_EPS:C_EPS + 1])
                inc(last, "S")

            @blk.vector
            def _(v):
                wait(v, "S")
                v.reciprocal(rb[:], rb[:])
                for j in range(ND):
                    v.scalar_tensor_tensor(tb[:], mb[:], -1.0, src_ap(j, cs),
                                           ALU.mult, ALU.add)
                    last = v.tensor_tensor(xnb[:, j * T + c0:j * T + c0 + 512],
                                           tb[:], rb[:], ALU.mult)
                inc(last, "V")

    with ExitStack() as p0:
        pr = p0.enter_context
        stgF = pr(nc.sbuf_tensor("stgF", [128, 2 * FF], F32, side="right"))
        stgB = pr(nc.sbuf_tensor("stgB", [128, 2 * FF], BF16, side="right"))
        qb = pr(nc.sbuf_tensor("qb", [128, FF], F32, side="right"))
        qbf = pr(nc.sbuf_tensor("qbf", [128, FF], BF16, side="right"))

        @blk.sync
        def _(q):
            for j in range(ND):
                q.dma_start(out=X1[:, j * T:(j + 1) * T],
                            in_=d_x[j * 128:(j + 1) * 128, :]).then_inc(sems["LD"], 16)
            q.dma_start(out=colsF[:, C_BZ:C_BZ + 8], in_=d_bz).then_inc(sems["LD"], 16)
            q.dma_start(out=colsF[:, C_BH:C_BH + 8], in_=d_bh).then_inc(sems["LD"], 16)
            q.dma_start(out=colsF[:, C_LOGD:C_LOGD + 8], in_=d_logd).then_inc(sems["LD"], 16)
            q.dma_start(out=colsF[:, C_BFC:C_BFC + 32], in_=d_bfc).then_inc(sems["LD"], 16)
            q.dma_start(out=colsF[:, C_BPJ:C_BPJ + 8], in_=d_bproj).then_inc(sems["LD"], 16)
            cnt["LD"] += 16 * (ND + 5)

        @blk.gpsimd
        def _(g):
            for j in range(ND):
                g.dma_start(out=afcB[:, j * 16:(j + 1) * 16],
                            in_=d_afcT[j * 128:(j + 1) * 128, :]).then_inc(sems["GD"], 16)
            for j in range(32):
                g.dma_start(out=aprojB[:, j * 16:(j + 1) * 16],
                            in_=d_aprojT[j * 128:(j + 1) * 128, :]).then_inc(sems["GD"], 16)
            cnt["GD"] += 16 * (ND + 32)

        @blk.vector
        def _(v):
            v.memset(colsF[:, C_EPS:C_EPS + 1], 1e-5)
            v.memset(colsF[:, C_ONE:C_ONE + 1], 1.0)
            v.memset(colsF[:, C_INVD:C_INVD + 1], 1.0 / D)
            v.memset(colsF[:, C_NEG1:C_NEG1 + 1], -1.0)
            v.memset(colsF[:, C_127:C_127 + 1], 127.0)
            v.memset(colsF[:, C_R127:C_R127 + 1], 1.0 / 127.0)
            v.memset(onesB[:], 1.0)
            v.memset(selB[:], 0.0)
            wait(v, "LD")
            last = v.tensor_copy(colsF[:, C_DCOEF:C_DCOEF + 8],
                                 colsF[:, C_LOGD:C_LOGD + 8])
            inc(last, "V")

        @blk.scalar
        def _(s):
            wait(s, "V")
            last = s.activation(colsF[:, C_DCOEF:C_DCOEF + 8],
                                colsF[:, C_DCOEF:C_DCOEF + 8], AF.Exp)
            inc(last, "S")

        @blk.sync
        def _(q):
            wait(q, "V")
            for s_ in range(16):
                q.dma_start(out=selB[s_:s_ + 1, s_ * 128:(s_ + 1) * 128],
                            in_=onesB[0:1, 0:128]).then_inc(sems["LD"], 16)
            cnt["LD"] += 16 * 16

        @blk.sync
        def _(q):
            q.dma_start(out=stgF[:16, 0:FF], in_=d_bfcT).then_inc(sems["LD"], 16)
            q.dma_start(out=stgF[:16, FF:FF + D], in_=d_bprojT).then_inc(sems["LD"], 16)
            cnt["LD"] += 32

        @blk.vector
        def _(v):
            wait(v, "LD")
            v.tensor_scalar(bfc2B[:], stgF[:16, 0:FF], 2.0, None, ALU.mult)
            last = v.tensor_scalar(bproj2B[:], stgF[:16, FF:FF + D], 2.0,
                                   None, ALU.mult)
            inc(last, "V")
        barrier()

        def quant(src, nptile, nslab, blkdst, qcol):
            w = nslab * 128
            ventry = cnt["V"]
            ld0 = cnt["LD"]
            v0 = cnt["V"]

            @blk.sync
            def _(q):
                for k in range(nptile):
                    if k < 2:
                        q.wait_ge(sems["V"], ventry)
                    else:
                        q.wait_ge(sems["V"], v0 + k - 1)
                    q.dma_start(out=stgF[:, (k % 2) * w:(k % 2) * w + w],
                                in_=src[k * 128:(k + 1) * 128, :]).then_inc(sems["LD"], 16)
                cnt["LD"] += 16 * nptile

            @blk.vector
            def _(v):
                for k in range(nptile):
                    v.wait_ge(sems["LD"], ld0 + 16 * (k + 1))
                    last = v.tensor_reduce(mcol[:, k:k + 1],
                                           stgF[:, (k % 2) * w:(k % 2) * w + w],
                                           AX.X, ALU.max,
                                           apply_absolute_value=True)
                    last.then_inc(sems["V"], 1)
                    cnt["V"] += 1
                    v.wait_ge(sems["V"], cnt["V"])
                last = v.tensor_reduce(mcol[:, 32:33], mcol[:, 0:nptile],
                                       AX.X, ALU.max)
                inc(last, "V")
                wait(v, "V")

            @blk.sync
            def _(q):
                wait(q, "V")
                q.dma_start(out=qb[0:1, 0:128],
                            in_=mcol[:, 32:33]).then_inc(sems["LD"], 16)
                cnt["LD"] += 16

            @blk.vector
            def _(v):
                wait(v, "LD")
                v.memset(qb[0:1, 256:384], 1.0)
                last = v.tensor_reduce(qb[0:1, 128:129], qb[0:1, 0:128],
                                       AX.X, ALU.max)
                inc(last, "V")
                wait(v, "V")
                bsc = qb[0:1, 128:129].broadcast_to([1, 128])
                last = v.tensor_tensor(qb[0:1, 384:512], bsc, bsc, ALU.max)
                inc(last, "V")

            @blk.tensor
            def _(tE):
                wait(tE, "V")
                tE.matmul(psA[:, 0:128], qb[0:1, 256:384],
                          qb[0:1, 384:512], start=True, stop=True)
                inc(tdrain(tE), "T")

            @blk.vector
            def _(v):
                wait(v, "T")
                negb = colsF[:, C_NEG1:C_NEG1 + 1].broadcast_to([128, 128])
                q0 = qcol
                v.tensor_tensor(qsc[:, q0:q0 + 128], psA[:, 0:128], negb,
                                ALU.max)
                v.reciprocal(qsc[:, q0 + 128:q0 + 256], qsc[:, q0:q0 + 128])
                v.tensor_tensor(
                    qsc[:, q0 + 128:q0 + 256], qsc[:, q0 + 128:q0 + 256],
                    colsF[:, C_127:C_127 + 1].broadcast_to([128, 128]),
                    ALU.mult)
                last = v.tensor_tensor(
                    qsc[:, q0:q0 + 128], qsc[:, q0:q0 + 128],
                    colsF[:, C_R127:C_R127 + 1].broadcast_to([128, 128]),
                    ALU.mult)
                inc(last, "V")
                wait(v, "V")

            if DBGQUANTP1ONLY:
                return
            ld1 = cnt["LD"]
            v1 = cnt["V"]
            sd0 = cnt["SD"]

            @blk.sync
            def _(q):
                for k in range(nptile):
                    if k < 2:
                        q.wait_ge(sems["V"], v1)
                    else:
                        q.wait_ge(sems["V"], v1 + 3 * (k - 1))
                    q.dma_start(out=stgF[:, (k % 2) * w:(k % 2) * w + w],
                                in_=src[k * 128:(k + 1) * 128, :]).then_inc(sems["LD"], 16)
                cnt["LD"] += 16 * nptile

            @blk.vector
            def _(v):
                v.wait_ge(sems["V"], v1)
                for k in range(nptile):
                    v.wait_ge(sems["LD"], ld1 + 16 * (k + 1))
                    if k >= 1:
                        v.wait_ge(sems["SD"], sd0 + 16 * nslab * k)
                    sl = stgF[:, (k % 2) * w:(k % 2) * w + w]
                    v.tensor_scalar(qb[:, 0:w], sl,
                                    qsc[:, qcol + 128:qcol + 129],
                                    MAGIC, ALU.mult, ALU.add)
                    v.tensor_scalar(qb[:, 0:w], qb[:, 0:w], MAGIC, -128.0,
                                    ALU.subtract, ALU.max)
                    last = v.tensor_scalar(qbf[:, 0:w], qb[:, 0:w], 127.0,
                                           qsc[:, qcol:qcol + 1],
                                           ALU.min, ALU.mult)
                    last.then_inc(sems["V"], 3)
                    cnt["V"] += 3

            @blk.scalar
            def _(s):
                for k in range(nptile):
                    s.wait_ge(sems["V"], v1 + 3 * (k + 1))
                    for m in range(nslab):
                        s.dma_start(out=blkdst[m * 128:(m + 1) * 128,
                                               k * 128:(k + 1) * 128],
                                    in_=qbf[:, m * 128:(m + 1) * 128]
                                    ).then_inc(sems["SD"], 16)
                    cnt["SD"] += 16 * nslab

        if not DBGNOQUANT:
            quant(d_wfcT, 8, 32, d_wqfcB, 0)
            quant(d_wprojT, 32, 8, d_wqpjB, 256)

        jobs = []
        for l in range(NLAYER):
            for k in range(8):
                jobs.append((d_inwT[l][k * 128:(k + 1) * 128, :], 128, 4096,
                             [(d_inwB[l][m * 128:(m + 1) * 128,
                                         k * 128:(k + 1) * 128], m * 128)
                              for m in range(32)]))
            for k in range(16):
                jobs.append((d_opwT[l][k * 128:(k + 1) * 128, :], 128, 1024,
                             [(d_opwB[l][m * 128:(m + 1) * 128,
                                         k * 128:(k + 1) * 128], m * 128)
                              for m in range(8)]))
            for k in range(16):
                jobs.append((d_xpwT[l][k * 128:(k + 1) * 128, :], 128, 96,
                             [(d_xpwB[l][k * 128:(k + 1) * 128, :], 0)]))
            jobs.append((d_dtwT[l], 64, 2048, [(d_dtwB[l], 0)]))

        gd0 = cnt["GD"]
        sd0 = cnt["SD"]
        cum_sd = []
        acc = sd0
        for (_, _, _, outs) in jobs:
            acc += 16 * len(outs)
            cum_sd.append(acc)

        @blk.gpsimd
        def _(g):
            for i, (srcap, p, w, outs) in enumerate(jobs):
                if i >= 2:
                    g.wait_ge(sems["SD"], cum_sd[i - 2])
                g.dma_start(out=stgB[0:p, (i % 2) * FF:(i % 2) * FF + w],
                            in_=srcap).then_inc(sems["GD"], 16)
                cnt["GD"] += 16

        @blk.scalar
        def _(s):
            for i, (srcap, p, w, outs) in enumerate(jobs):
                s.wait_ge(sems["GD"], gd0 + 16 * (i + 1))
                for dst, c0 in outs:
                    ww = dst.shape[1]
                    s.dma_start(out=dst,
                                in_=stgB[0:p, (i % 2) * FF + c0:
                                         (i % 2) * FF + c0 + ww]
                                ).then_inc(sems["SD"], 16)
                cnt["SD"] += 16 * len(outs)
        barrier()

    if upto == "p0":
        @blk.sync
        def _(q):
            for nm in ("T", "S", "V", "GD", "SD"):
                q.wait_ge(sems[nm], cnt[nm])
            q.dma_start(out=d_y[0:128, 0:192], in_=colsF[:]).then_inc(sems["LD"], 16)
            cnt["LD"] += 16
            wait(q, "LD")

        @blk.gpsimd
        def _(g):
            wait(g, "LD")
            g.dma_start(out=d_y[0:128, 192:1216],
                        in_=d_wqfcB[0:128, :]).then_inc(sems["GD"], 16)
            g.dma_start(out=d_y[0:128, 1216:1312],
                        in_=d_xpwB[0][0:128, :]).then_inc(sems["GD"], 16)
            g.dma_start(out=d_y[128:256, 0:1024],
                        in_=d_inwB[0][0:128, :]).then_inc(sems["GD"], 16)
            g.dma_start(out=d_y[256:384, 0:2048],
                        in_=d_opwB[0][0:128, :]).then_inc(sems["GD"], 16)
            cnt["GD"] += 64
            wait(g, "GD")
        x1_stack.close()
        ctx.close()
        return nc

    with ExitStack() as p1:
        pr = p1.enter_context
        wzB = pr(nc.sbuf_tensor("wzB", [128, ND * D], BF16, side="right"))
        whB = pr(nc.sbuf_tensor("whB", [128, ND * D], BF16, side="right"))

        @blk.gpsimd
        def _(g):
            for j in range(ND):
                g.dma_start(out=wzB[:, j * D:(j + 1) * D],
                            in_=d_wz[j * 128:(j + 1) * 128, :]).then_inc(sems["GD"], 16)
                g.dma_start(out=whB[:, j * D:(j + 1) * D],
                            in_=d_wh[j * 128:(j + 1) * 128, :]).then_inc(sems["GD"], 16)
            cnt["GD"] += 16 * 2 * ND

        with ExitStack() as lnscope:
            lr = lnscope.enter_context
            scr = (lr(nc.sbuf_tensor("a_xb", [128, ND * 512], BF16, side="right")),
                   lr(nc.sbuf_tensor("a_sq", [128, ND * 512], BF16, side="right")),
                   lr(nc.sbuf_tensor("a_m", [128, 512], F32, side="right")),
                   lr(nc.sbuf_tensor("a_r", [128, 512], F32, side="right")),
                   lr(nc.sbuf_tensor("a_t", [128, 512], F32, side="right")))
            emit_ln(lambda j, cs: X1[:, j * T:(j + 1) * T][:, cs], scr)
            barrier()

        HH = 4
        TCH = TCA + HH
        zb = pr(nc.sbuf_tensor("zb", [128, ND * TCA], F32, side="right"))
        htb = pr(nc.sbuf_tensor("htb", [128, ND * TCA], F32, side="right"))
        asb = pr(nc.sbuf_tensor("asb", [128, ND * TCH], F32, side="right"))
        bsb = pr(nc.sbuf_tensor("bsb", [128, ND * TCH], F32, side="right"))
        hcc = pr(nc.sbuf_tensor("hcc", [128, ND * TCH], F32, side="right"))
        hlb = pr(nc.sbuf_tensor("hlb", [128, ND * DCOLS], F32, side="right"))
        hrb = pr(nc.sbuf_tensor("hrb", [128, ND * DCOLS], F32, side="right"))
        dfb = pr(nc.sbuf_tensor("dfb", [128, DCOLS], F32, side="right"))

        @blk.vector
        def _(v):
            v.memset(asb[:], 0.0)
            last = v.memset(bsb[:], 0.0)
            inc(last, "V")

        for cc in range(T // TCA):
            cs = slice(cc * TCA, (cc + 1) * TCA)
            items = []
            for j in range(ND):
                def mk_mm(j=j, cs=cs):
                    def f(tE, ps):
                        for k in range(ND):
                            last = tE.matmul(
                                ps[:, 0:TCA],
                                wzB[:, k * D + j * 128:k * D + (j + 1) * 128],
                                xnb[:, k * T:(k + 1) * T][:, cs],
                                start=(k == 0), stop=(k == ND - 1))
                        return last
                    return f

                def mk_ev(j=j):
                    def f(sE, ps):
                        return sE.activation(zb[:, j * TCA:(j + 1) * TCA],
                                             ps[:, 0:TCA], AF.Sigmoid,
                                             bias=colsF[:, C_BZ + j:C_BZ + j + 1])
                    return f
                items.append({"mms": mk_mm(), "evac": mk_ev()})
            for j in range(ND):
                def mk_mm(j=j, cs=cs):
                    def f(tE, ps):
                        for k in range(ND):
                            last = tE.matmul(
                                ps[:, 0:TCA],
                                whB[:, k * D + j * 128:k * D + (j + 1) * 128],
                                xnb[:, k * T:(k + 1) * T][:, cs],
                                start=(k == 0), stop=(k == ND - 1))
                        return last
                    return f

                def mk_ev(j=j):
                    def f(sE, ps):
                        return sE.activation(htb[:, j * TCA:(j + 1) * TCA],
                                             ps[:, 0:TCA], AF.Identity,
                                             bias=colsF[:, C_BH + j:C_BH + j + 1])
                    return f
                items.append({"mms": mk_mm(), "evac": mk_ev()})
            if cc == 0:
                @blk.tensor
                def _(tE):
                    wait(tE, "GD")
            mm_stage(items, [psA, psB])
            barrier()

            @blk.vector
            def _(v):
                for j in range(ND):
                    sj = slice(j * TCA, (j + 1) * TCA)
                    b0 = j * TCH
                    v.tensor_scalar(asb[:, b0 + HH:b0 + TCH], zb[:, sj],
                                    -1.0, 1.0, ALU.mult, ALU.add)
                    last = v.tensor_tensor(bsb[:, b0 + HH:b0 + TCH], zb[:, sj],
                                           htb[:, sj], ALU.mult)
                    inc(last, "V")
                    wait(v, "V")
                    last = v.tensor_tensor_scan(hcc[:, b0:b0 + TCH],
                                                asb[:, b0:b0 + TCH],
                                                bsb[:, b0:b0 + TCH],
                                                0.0, ALU.mult, ALU.add)
                    inc(last, "V")
                    wait(v, "V")
                    lc = hcc[:, b0 + TCH - HH:b0 + TCH]
                    last = v.tensor_tensor(bsb[:, b0:b0 + HH], lc, lc, ALU.max)
                inc(last, "V")
            barrier()

            for d0 in range(0, TCA, DCOLS):
                @blk.sync
                def _(q):
                    wait(q, "V")
                    for j in range(ND):
                        sj = slice(j * TCH + HH + d0, j * TCH + HH + d0 + DCOLS)
                        jm, jp = (j - 1) % ND, (j + 1) % ND
                        sjm = slice(jm * TCH + HH + d0, jm * TCH + HH + d0 + DCOLS)
                        sjp = slice(jp * TCH + HH + d0, jp * TCH + HH + d0 + DCOLS)
                        dj = slice(j * DCOLS, (j + 1) * DCOLS)
                        q.dma_start(out=hlb[:, dj][1:128, :],
                                    in_=hcc[:, sj][0:127, :]).then_inc(sems["LD"], 16)
                        q.dma_start(out=hlb[:, dj][0:1, :],
                                    in_=hcc[:, sjm][127:128, :]).then_inc(sems["LD"], 16)
                        q.dma_start(out=hrb[:, dj][0:127, :],
                                    in_=hcc[:, sj][1:128, :]).then_inc(sems["LD"], 16)
                        q.dma_start(out=hrb[:, dj][127:128, :],
                                    in_=hcc[:, sjp][0:1, :]).then_inc(sems["LD"], 16)
                    cnt["LD"] += 16 * 4 * ND

                @blk.vector
                def _(v):
                    wait(v, "LD")
                    for j in range(ND):
                        sj = slice(j * TCH + HH + d0, j * TCH + HH + d0 + DCOLS)
                        dj = slice(j * DCOLS, (j + 1) * DCOLS)
                        v.tensor_tensor(dfb[:], hlb[:, dj], hrb[:, dj], ALU.add)
                        v.scalar_tensor_tensor(dfb[:], hcc[:, sj], -2.0, dfb[:],
                                               ALU.mult, ALU.add)
                        v.scalar_tensor_tensor(
                            dfb[:], dfb[:], colsF[:, C_DCOEF + j:C_DCOEF + j + 1],
                            hcc[:, sj], ALU.mult, ALU.add)
                        xj = X1[:, j * T:(j + 1) * T][:, cc * TCA + d0:
                                                      cc * TCA + d0 + DCOLS]
                        last = v.tensor_tensor(xj, xj, dfb[:], ALU.add)
                    inc(last, "V")
                barrier()
        if upto == "p1dbg":
            @blk.sync
            def _(q):
                for nm in ("T", "S", "V", "GD", "SD"):
                    q.wait_ge(sems[nm], cnt[nm])
                q.dma_start(out=d_y[0:128, :], in_=hcc[:, 0:2048]).then_inc(sems["LD"], 16)
                q.dma_start(out=d_y[128:256, :], in_=asb[:, 0:2048]).then_inc(sems["LD"], 16)
                q.dma_start(out=d_y[256:384, :], in_=bsb[:, 0:2048]).then_inc(sems["LD"], 16)
                q.dma_start(out=d_y[384:512, :], in_=zb[:, 0:2048]).then_inc(sems["LD"], 16)
                q.dma_start(out=d_y[512:640, :], in_=htb[:, 0:2048]).then_inc(sems["LD"], 16)
                q.dma_start(out=d_y[768:896, :], in_=X1[:, 0:2048]).then_inc(sems["LD"], 16)
                cnt["LD"] += 16 * 6
                wait(q, "LD")

            @blk.gpsimd
            def _(g):
                wait(g, "LD")
                g.dma_start(out=d_y[640:768, :], in_=xnb[:, 0:2048]).then_inc(sems["GD"], 16)
                cnt["GD"] += 16
                wait(g, "GD")

    def tap_X1_and_finish():
        @blk.sync
        def _(q):
            for nm in ("T", "S", "V", "GD", "SD"):
                q.wait_ge(sems[nm], cnt[nm])
            for j in range(ND):
                q.dma_start(out=d_y[j * 128:(j + 1) * 128, :],
                            in_=X1[:, j * T:(j + 1) * T]).then_inc(sems["LD"], 16)
            cnt["LD"] += 16 * ND
            wait(q, "LD")

    def tap_xafter_and_finish():
        @blk.sync
        def _(q):
            for nm in ("T", "S", "V", "GD", "SD"):
                q.wait_ge(sems[nm], cnt[nm])
            q.dma_start(out=d_y, in_=d_xafter).then_inc(sems["LD"], 16)
            cnt["LD"] += 16
            wait(q, "LD")

    def ffn_chunks(sink_tensor, bufs):
        sink = "X1"
        X1s = sink_tensor
        h1b, lorab, lorab2, wfcbuf, wpjbuf = bufs
        for cc in range(T // TCF):
            cs = slice(cc * TCF, (cc + 1) * TCF)

            @blk.tensor
            def _(tE):
                wait(tE, "V")
                for k in range(ND):
                    tE.matmul(psL16[:, 0:TCF],
                              afcB[:, k * 16:(k + 1) * 16],
                              xnb[:, k * T:(k + 1) * T][:, cs],
                              start=(k == 0), stop=(k == ND - 1))
                inc(tdrain(tE), "T")

            @blk.scalar
            def _(s):
                wait(s, "T")
                last = s.copy(lorab[:], psL16[:, 0:TCF])
                inc(last, "S")

            items = []
            for m in range(32):
                def mk_load(m=m):
                    def f(q):
                        q.dma_start(out=wfcbuf[:, (m % 2) * 1024:(m % 2 + 1) * 1024],
                                    in_=d_wqfcB[m * 128:(m + 1) * 128, :]
                                    ).then_inc(sems["LD"], 16)
                        return 1
                    return f

                def mk_mm(m=m, cs=cs):
                    def f(tE, ps):
                        lastk = None
                        for k in range(ND):
                            lastk = tE.matmul(ps[:, 0:TCF],
                                      wfcbuf[:, (m % 2) * 1024 + k * 128:
                                             (m % 2) * 1024 + (k + 1) * 128],
                                      xnb[:, k * T:(k + 1) * T][:, cs],
                                      start=(k == 0),
                                      stop=(DBGNOLORA and k == ND - 1))
                        if DBGNOLORA:
                            return lastk
                        return tE.matmul(ps[:, 0:TCF],
                                         bfc2B[:, m * 128:(m + 1) * 128],
                                         lorab[:], start=False, stop=True)
                    return f

                def mk_ev(m=m):
                    def f(sE, ps):
                        return sE.activation(h1b[:, m * TCF:(m + 1) * TCF],
                                             ps[:, 0:TCF], AF.Gelu,
                                             bias=colsF[:, C_BFC + m:C_BFC + m + 1])
                    return f
                items.append({"mms": mk_mm(), "evac": mk_ev(), "load": mk_load()})
            mm_stage(items, [psA, psB])
            barrier()

            @blk.tensor
            def _(tE):
                for k in range(32):
                    tE.matmul(psL16[:, 0:TCF],
                              aprojB[:, k * 16:(k + 1) * 16],
                              h1b[:, k * TCF:(k + 1) * TCF],
                              start=(k == 0), stop=(k == 31))
                inc(tdrain(tE), "T")

            @blk.scalar
            def _(s):
                wait(s, "T")
                last = s.copy(lorab2[:], psL16[:, 0:TCF])
                inc(last, "S")

            items = []
            for m in range(ND):
                def mk_load(m=m):
                    def f(q):
                        q.dma_start(out=wpjbuf[:, (m % 2) * 4096:(m % 2 + 1) * 4096],
                                    in_=d_wqpjB[m * 128:(m + 1) * 128, :]
                                    ).then_inc(sems["LD"], 16)
                        return 1
                    return f

                def mk_mm(m=m):
                    def f(tE, ps):
                        lastk = None
                        for k in range(32):
                            lastk = tE.matmul(ps[:, 0:TCF],
                                      wpjbuf[:, (m % 2) * 4096 + k * 128:
                                             (m % 2) * 4096 + (k + 1) * 128],
                                      h1b[:, k * TCF:(k + 1) * TCF],
                                      start=(k == 0),
                                      stop=(DBGNOLORA and k == 31))
                        if DBGNOLORA:
                            return lastk
                        return tE.matmul(ps[:, 0:TCF],
                                         bproj2B[:, m * 128:(m + 1) * 128],
                                         lorab2[:], start=False, stop=True)
                    return f

                def mk_ev(m=m, cs=cs):
                    def f(vE, ps):
                        xj = X1s[:, m * T:(m + 1) * T][:, cs]
                        return vE.scalar_tensor_tensor(
                            xj, ps[:, 0:TCF],
                            colsF[:, C_BPJ + m:C_BPJ + m + 1], xj,
                            ALU.add, ALU.add)
                    return f
                items.append({"mms": mk_mm(), "evac": mk_ev(), "load": mk_load()})
            mm_stage(items, [psC, psD], evac_eng="V")
            barrier()

    if upto == "p1dbg":
        x1_stack.close()
        ctx.close()
        return nc

    if upto == "p1":
        tap_X1_and_finish()
        x1_stack.close()
        ctx.close()
        return nc

    with ExitStack() as p2:
        pr = p2.enter_context
        with ExitStack() as lnscope:
            lr = lnscope.enter_context
            scr = (lr(nc.sbuf_tensor("f_xb", [128, ND * 512], BF16, side="right")),
                   lr(nc.sbuf_tensor("f_sq", [128, ND * 512], BF16, side="right")),
                   lr(nc.sbuf_tensor("f_m", [128, 512], F32, side="right")),
                   lr(nc.sbuf_tensor("f_r", [128, 512], F32, side="right")),
                   lr(nc.sbuf_tensor("f_t", [128, 512], F32, side="right")))
            emit_ln(lambda j, cs: X1[:, j * T:(j + 1) * T][:, cs], scr)
            barrier()
        bufs = (pr(nc.sbuf_tensor("f_h1", [128, 32 * TCF], BF16, side="right")),
                pr(nc.sbuf_tensor("f_lr", [16, TCF], BF16, side="right")),
                pr(nc.sbuf_tensor("f_lr2", [16, TCF], BF16, side="right")),
                pr(nc.sbuf_tensor("f_wfc", [128, 2 * 1024], BF16, side="right")),
                pr(nc.sbuf_tensor("f_wpj", [128, 2 * 4096], BF16, side="right")))
        ffn_chunks(X1, bufs)

    if upto == "p2":
        tap_X1_and_finish()
        x1_stack.close()
        ctx.close()
        return nc

    with ExitStack() as p3ln:
        lr = p3ln.enter_context
        scr = (lr(nc.sbuf_tensor("m_xb", [128, ND * 512], BF16, side="right")),
               lr(nc.sbuf_tensor("m_sq", [128, ND * 512], BF16, side="right")),
               lr(nc.sbuf_tensor("m_m", [128, 512], F32, side="right")),
               lr(nc.sbuf_tensor("m_r", [128, 512], F32, side="right")),
               lr(nc.sbuf_tensor("m_t", [128, 512], F32, side="right")))
        emit_ln(lambda j, cs: X1[:, j * T:(j + 1) * T][:, cs], scr)
        barrier()

        @blk.sync
        def _(q):
            for j in range(ND):
                q.dma_start(out=d_xspill[j * 128:(j + 1) * 128, :],
                            in_=X1[:, j * T:(j + 1) * T]).then_inc(sems["LD"], 16)
            cnt["LD"] += 16 * ND
        barrier()
    x1_stack.close()

    with ExitStack() as p3:
        pr = p3.enter_context
        xib = pr(nc.sbuf_tensor("xib", [128, NDI * (TC + 8)], BF16, side="right"))
        gb = pr(nc.sbuf_tensor("gb", [128, NDI * TC], BF16, side="right"))
        xcb = pr(nc.sbuf_tensor("xcb", [128, NDI * TC], BF16, side="right"))
        brep = pr(nc.sbuf_tensor("brep", [128, NS * TC], BF16, side="right"))
        crep = pr(nc.sbuf_tensor("crep", [128, NS * TC], BF16, side="right"))
        HH = 4
        TCH = TC + HH
        hgb = pr(nc.sbuf_tensor("hgb", [128, SG * TCH], F32, side="right"))
        hcg = pr(nc.sbuf_tensor("hcg", [128, SG * TC], F32, side="right"))
        treb = pr(nc.sbuf_tensor("treb", [128, 2 * TC], F32, side="right"))
        delb = pr(nc.sbuf_tensor("delb", [128, 2 * TC], F32, side="right"))
        dub = pr(nc.sbuf_tensor("dub", [128, TC], F32, side="right"))
        dab = pr(nc.sbuf_tensor("dab", [128, 2 * TCH], F32, side="right"))
        dbub = pr(nc.sbuf_tensor("dbub", [128, NS * TCH], F32, side="right"))
        yb = pr(nc.sbuf_tensor("yb", [128, TC], F32, side="right"))
        ydb = pr(nc.sbuf_tensor("ydb", [128, TC], F32, side="right"))
        dtbuf = pr(nc.sbuf_tensor("dtbuf", [64, TC], BF16, side="right"))
        bcb = pr(nc.sbuf_tensor("bcb", [16, 2 * TC], BF16, side="right"))
        spt = pr(nc.sbuf_tensor("spt", [128, 4 * TC], F32, side="right"))
        inwbuf = pr(nc.sbuf_tensor("inwbuf", [128, 2 * 1024], BF16, side="right"))
        opwbuf = pr(nc.sbuf_tensor("opwbuf", [128, 2 * 2048], BF16, side="right"))
        xpwb = pr(nc.sbuf_tensor("xpwb", [128, NDI * 96], BF16, side="right"))
        dtwb = pr(nc.sbuf_tensor("dtwb", [64, DI], BF16, side="right"))
        xstr = pr(nc.sbuf_tensor("xstr", [128, ND * TC], F32, side="right"))
        otb = pr(nc.sbuf_tensor("otb", [128, ND * TC], F32, side="right"))

        nlayer_run = _NL[0] if upto == "p3dbg" else NLAYER
        for layer in range(nlayer_run):
            is_last = layer == nlayer_run - 1

            @blk.sync
            def _(q):
                q.dma_start(out=cwF[:], in_=d_cw[layer]).then_inc(sems["LD"], 16)
                q.dma_start(out=colsF[:, C_CB:C_CB + 16],
                            in_=d_cb[layer]).then_inc(sems["LD"], 16)
                q.dma_start(out=colsF[:, C_DTB:C_DTB + 16],
                            in_=d_dtb[layer]).then_inc(sems["LD"], 16)
                q.dma_start(out=colsF[:, C_DSK:C_DSK + 16],
                            in_=d_dsk[layer]).then_inc(sems["LD"], 16)
                q.dma_start(out=acolF[:], in_=d_alog[layer]).then_inc(sems["LD"], 16)
                for k in range(NDI):
                    q.dma_start(out=xpwb[:, k * 96:(k + 1) * 96],
                                in_=d_xpwB[layer][k * 128:(k + 1) * 128, :]
                                ).then_inc(sems["LD"], 16)
                q.dma_start(out=dtwb[:], in_=d_dtwB[layer]).then_inc(sems["LD"], 16)
                cnt["LD"] += 16 * (6 + NDI)

            @blk.scalar
            def _(s):
                wait(s, "LD")
                last = s.activation(acolF[:], acolF[:], AF.Exp)
                inc(last, "S")

            @blk.vector
            def _(v):
                wait(v, "S")
                v.tensor_scalar(acolF[:], acolF[:], -1.0, None, ALU.mult)
                v.memset(xib[:], 0.0)
                v.memset(dbub[:], 0.0)
                last = v.memset(dab[:], 0.0)
                inc(last, "V")
            barrier()

            for cc in range(NCH):
                cs = slice(cc * TC, (cc + 1) * TC)

                if cc > 0:
                    @blk.vector
                    def _(v):
                        wait(v, "S")
                        for j in range(NDI):
                            b0 = j * (TC + 8)
                            dsth = xib[:, b0:b0 + 8].bitcast(mybir.dt.uint32)
                            srch = xib[:, b0 + TC:b0 + TC + 8].bitcast(
                                mybir.dt.uint32)
                            last = v.tensor_tensor(dsth, srch, srch,
                                                   ALU.bitwise_or)
                        inc(last, "V")

                    @blk.scalar
                    def _(s):
                        wait(s, "V")
                barrier()

                items = []
                for m in range(32):
                    def mk_load(layer=layer, m=m):
                        def f(q):
                            q.dma_start(out=inwbuf[:, (m % 2) * 1024:(m % 2 + 1) * 1024],
                                        in_=d_inwB[layer][m * 128:(m + 1) * 128, :]
                                        ).then_inc(sems["LD"], 16)
                            return 1
                        return f

                    def mk_mm(m=m, cs=cs):
                        def f(tE, ps):
                            for k in range(ND):
                                last = tE.matmul(
                                    ps[:, 0:TC],
                                    inwbuf[:, (m % 2) * 1024 + k * 128:
                                           (m % 2) * 1024 + (k + 1) * 128],
                                    xnb[:, k * T:(k + 1) * T][:, cs],
                                    start=(k == 0), stop=(k == ND - 1))
                            return last
                        return f

                    if m < 16:
                        def mk_ev(m=m):
                            def f(sE, ps):
                                b0 = m * (TC + 8)
                                return sE.copy(xib[:, b0 + 8:b0 + 8 + TC],
                                               ps[:, 0:TC])
                            return f
                    else:
                        def mk_ev(m=m):
                            def f(sE, ps):
                                return sE.activation(
                                    gb[:, (m - 16) * TC:(m - 15) * TC],
                                    ps[:, 0:TC], AF.Silu)
                            return f
                    items.append({"mms": mk_mm(), "evac": mk_ev(),
                                  "load": mk_load()})
                mm_stage(items, [psA, psB])
                barrier()

                v0c2 = cnt["V"]
                s0c2 = cnt["S"]

                @blk.vector
                def _(v):
                    for j in range(NDI):
                        b0 = j * (TC + 8)
                        if j >= 2:
                            v.wait_ge(sems["S"], s0c2 + j - 1)
                        sl = spt[:, (j % 2) * TC:(j % 2 + 1) * TC]
                        v.tensor_scalar(sl, xib[:, b0 + 5:b0 + 5 + TC],
                                        cwF[:, j * 4:j * 4 + 1], None, ALU.mult)
                        for tp in range(1, 4):
                            last = v.scalar_tensor_tensor(
                                sl, xib[:, b0 + 5 + tp:b0 + 5 + tp + TC],
                                cwF[:, j * 4 + tp:j * 4 + tp + 1], sl,
                                ALU.mult, ALU.add)
                        last.then_inc(sems["V"], 1)
                        cnt["V"] += 1

                @blk.scalar
                def _(s):
                    for j in range(NDI):
                        s.wait_ge(sems["V"], v0c2 + j + 1)
                        last = s.activation(xcb[:, j * TC:(j + 1) * TC],
                                            spt[:, (j % 2) * TC:(j % 2 + 1) * TC],
                                            AF.Silu,
                                            bias=colsF[:, C_CB + j:C_CB + j + 1])
                        last.then_inc(sems["S"], 1)
                        cnt["S"] += 1
                barrier()

                @blk.tensor
                def _(tE):
                    for k in range(NDI):
                        tE.matmul(psS96[0:64, 0:TC],
                                  xpwb[:, k * 96:k * 96 + 64],
                                  xcb[:, k * TC:(k + 1) * TC],
                                  start=(k == 0), stop=(k == NDI - 1))
                    for k in range(NDI):
                        tE.matmul(psL16[:, 0:TC],
                                  xpwb[:, k * 96 + 64:k * 96 + 80],
                                  xcb[:, k * TC:(k + 1) * TC],
                                  start=(k == 0), stop=(k == NDI - 1))
                    for k in range(NDI):
                        tE.matmul(psL16b[:, 0:TC],
                                  xpwb[:, k * 96 + 80:k * 96 + 96],
                                  xcb[:, k * TC:(k + 1) * TC],
                                  start=(k == 0), stop=(k == NDI - 1))
                    inc(tdrain(tE), "T")

                @blk.scalar
                def _(s):
                    wait(s, "T")
                    s.copy(dtbuf[:], psS96[0:64, 0:TC])
                    s.copy(bcb[:, 0:TC], psL16[:, 0:TC])
                    last = s.copy(bcb[:, TC:2 * TC], psL16b[:, 0:TC])
                    inc(last, "S")
                barrier()

                items = []
                for which in range(2):
                    dstb = brep if which == 0 else crep
                    for s_ in range(NS):
                        def mk_mm(s_=s_, which=which):
                            def f(tE, ps):
                                return tE.matmul(ps[:, 0:TC],
                                                 selB[:, s_ * 128:(s_ + 1) * 128],
                                                 bcb[:, which * TC:(which + 1) * TC],
                                                 start=True, stop=True)
                            return f

                        def mk_ev(s_=s_, dstb=dstb):
                            def f(sE, ps):
                                return sE.copy(dstb[:, s_ * TC:(s_ + 1) * TC],
                                               ps[:, 0:TC])
                            return f
                        items.append({"mms": mk_mm(), "evac": mk_ev()})
                mm_stage(items, [psC, psD])
                barrier()

                vprot = {}
                for j in range(NDI):
                    pj = j % 2

                    @blk.tensor
                    def _(tE):
                        if j >= 2:
                            wait(tE, "V", vprot[j - 2])
                        tE.matmul((psA if pj == 0 else psB)[:, 0:TC],
                                  dtwb[:, j * 128:(j + 1) * 128],
                                  dtbuf[:], start=True, stop=True)
                        inc(tdrain(tE), "T")

                    @blk.scalar
                    def _(s):
                        wait(s, "T")
                        if j >= 1:
                            wait(s, "V", vprot.get(j - 1, 0))
                        ps = (psA if pj == 0 else psB)
                        sp0 = spt[:, 2 * TC:3 * TC]
                        sp1 = spt[:, 3 * TC:4 * TC]
                        s.activation(sp0, ps[:, 0:TC], AF.Abs,
                                     bias=colsF[:, C_DTB + j:C_DTB + j + 1])
                        s.activation(sp1, sp0, AF.Exp,
                                     scale=colsF[:, C_NEG1:C_NEG1 + 1])
                        s.activation(sp0, sp1, AF.Ln,
                                     bias=colsF[:, C_ONE:C_ONE + 1])
                        last = s.activation(sp1, ps[:, 0:TC], AF.Relu,
                                            bias=colsF[:, C_DTB + j:C_DTB + j + 1])
                        inc(last, "S")

                    @blk.vector
                    def _(v):
                        wait(v, "S")
                        dj = delb[:, pj * TC:(pj + 1) * TC]
                        v.tensor_tensor(dj, spt[:, 2 * TC:3 * TC],
                                        spt[:, 3 * TC:4 * TC], ALU.add)
                        last = v.tensor_tensor(dub[:], dj,
                                               xcb[:, j * TC:(j + 1) * TC],
                                               ALU.mult)
                        inc(last, "V")
                    vdel = cnt["V"]

                    s1 = cnt["S"]
                    v1 = cnt["V"]

                    @blk.scalar
                    def _(s):
                        for si in range(NS):
                            if si < 2:
                                s.wait_ge(sems["V"], v1)
                            else:
                                s.wait_ge(sems["V"], v1 + si)
                            last = s.activation(
                                dab[:, (si % 2) * TCH + HH:(si % 2 + 1) * TCH],
                                delb[:, pj * TC:(pj + 1) * TC], AF.Exp,
                                scale=acolF[:, j * NS + si:j * NS + si + 1])
                            last.then_inc(sems["S"], 1)
                            cnt["S"] += 1

                    @blk.vector
                    def _(v):
                        dbu_out = dbub[:].rearrange(
                            "p (s t) -> p s t", s=NS)[:, :, HH:TCH]
                        dub_b = dub[:].unsqueeze(1).broadcast_to([128, NS, TC])
                        brep_v = brep[:].rearrange("p (s t) -> p s t", s=NS)
                        last = v.tensor_tensor(dbu_out, dub_b, brep_v, ALU.mult)
                        last.then_inc(sems["V"], 1)
                        cnt["V"] += 1
                        v.wait_ge(sems["V"], cnt["V"])
                        for si in range(NS):
                            v.wait_ge(sems["S"], s1 + si + 1)
                            sseg = si * TCH
                            gslot = si % SG
                            gseg = gslot * TCH
                            last = v.tensor_tensor_scan(
                                hgb[:, gseg:gseg + TCH],
                                dab[:, (si % 2) * TCH:(si % 2 + 1) * TCH],
                                dbub[:, sseg:sseg + TCH], 0.0,
                                ALU.mult, ALU.add)
                            last.then_inc(sems["V"], 1)
                            cnt["V"] += 1
                            if gslot == SG - 1:
                                v.wait_ge(sems["V"], cnt["V"])
                                g0 = si - SG + 1
                                hsl = hgb[:].rearrange(
                                    "p (s t) -> p s t", s=SG)[:, :, TCH - HH:TCH]
                                dsl = dbub[:].rearrange(
                                    "p (s t) -> p s t", s=NS)[:, g0:g0 + SG, 0:HH]
                                v.tensor_tensor(dsl, hsl, hsl, ALU.max)
                                hgd = hgb[:].rearrange(
                                    "p (s t) -> p s t", s=SG)[:, :, HH:TCH]
                                v.tensor_tensor(
                                    hcg[:].rearrange("p (s t) -> p s t", s=SG),
                                    hgd,
                                    crep[:, g0 * TC:(g0 + SG) * TC].rearrange(
                                        "p (s t) -> p s t", s=SG),
                                    ALU.mult)
                                v.tensor_tensor(treb[:], hcg[:, 0:2 * TC],
                                                hcg[:, 2 * TC:4 * TC], ALU.add)
                                if g0 == 0:
                                    v.tensor_tensor(yb[:], treb[:, 0:TC],
                                                    treb[:, TC:2 * TC], ALU.add)
                                else:
                                    v.tensor_tensor(treb[:, 0:TC], treb[:, 0:TC],
                                                    treb[:, TC:2 * TC], ALU.add)
                                    v.tensor_tensor(yb[:], yb[:],
                                                    treb[:, 0:TC], ALU.add)
                        v.scalar_tensor_tensor(ydb[:], xcb[:, j * TC:(j + 1) * TC],
                                               colsF[:, C_DSK + j:C_DSK + j + 1],
                                               yb[:], ALU.mult, ALU.add)
                        last = v.tensor_tensor(xcb[:, j * TC:(j + 1) * TC],
                                               ydb[:], gb[:, j * TC:(j + 1) * TC],
                                               ALU.mult)
                        last.then_inc(sems["V"], 1)
                        cnt["V"] += 1
                    vprot[j] = cnt["V"]
                barrier()

                ldx0 = cnt["LD"]
                if is_last:
                    @blk.sync
                    def _(q):
                        wait(q, "GD")
                        for m in range(ND):
                            q.dma_start(out=xstr[:, m * TC:(m + 1) * TC],
                                        in_=d_xspill[m * 128:(m + 1) * 128, :][:, cs]
                                        ).then_inc(sems["LD"], 16)
                        cnt["LD"] += 16 * ND

                items = []
                for m in range(ND):
                    def mk_load(layer=layer, m=m):
                        def f(q):
                            q.dma_start(out=opwbuf[:, (m % 2) * 2048:(m % 2 + 1) * 2048],
                                        in_=d_opwB[layer][m * 128:(m + 1) * 128, :]
                                        ).then_inc(sems["LD"], 16)
                            return 1
                        return f

                    def mk_mm(m=m):
                        def f(tE, ps):
                            for k in range(NDI):
                                last = tE.matmul(
                                    ps[:, 0:TC],
                                    opwbuf[:, (m % 2) * 2048 + k * 128:
                                           (m % 2) * 2048 + (k + 1) * 128],
                                    xcb[:, k * TC:(k + 1) * TC],
                                    start=(k == 0), stop=(k == NDI - 1))
                            return last
                        return f

                    if not is_last:
                        def mk_ev(m=m, cs=cs):
                            def f(sE, ps):
                                return sE.copy(xnb[:, m * T:(m + 1) * T][:, cs],
                                               ps[:, 0:TC])
                            return f
                    else:
                        def mk_ev(m=m, ldx0=ldx0):
                            def f(sE, ps):
                                return sE.copy(otb[:, m * TC:(m + 1) * TC],
                                               ps[:, 0:TC])
                            return f
                    items.append({"mms": mk_mm(), "evac": mk_ev(),
                                  "load": mk_load()})
                mm_stage(items, [psC, psD])

                if is_last:
                    s0c6 = cnt["S"] - ND
                    ldx1 = ldx0 + 16 * ND

                    @blk.vector
                    def _(v):
                        v.wait_ge(sems["LD"], ldx1)
                        for m in range(ND):
                            v.wait_ge(sems["S"], s0c6 + m + 1)
                            last = v.tensor_tensor(xstr[:, m * TC:(m + 1) * TC],
                                                   xstr[:, m * TC:(m + 1) * TC],
                                                   otb[:, m * TC:(m + 1) * TC],
                                                   ALU.add)
                            last.then_inc(sems["V"], 1)
                            cnt["V"] += 1

                    vv0 = cnt["V"] - ND

                    @blk.gpsimd
                    def _(g):
                        for m in range(ND):
                            g.wait_ge(sems["V"], vv0 + m + 1)
                            g.dma_start(out=d_xafter[m * 128:(m + 1) * 128, :][:, cs],
                                        in_=xstr[:, m * TC:(m + 1) * TC]
                                        ).then_inc(sems["GD"], 16)
                        cnt["GD"] += 16 * ND
                barrier()

        if upto == "p3dbg":
            @blk.sync
            def _(q):
                for nm in ("T", "S", "V", "GD", "SD"):
                    q.wait_ge(sems[nm], cnt[nm])
                q.dma_start(out=d_y[512:640, 0:256], in_=acolF[:]).then_inc(sems["LD"], 16)
                q.dma_start(out=d_y[768:896, 0:1040], in_=hgb[:]).then_inc(sems["LD"], 16)
                q.dma_start(out=d_y[640:768, 0:512], in_=delb[:]).then_inc(sems["LD"], 16)
                cnt["LD"] += 16 * 3
                wait(q, "LD")

            @blk.gpsimd
            def _(g):
                wait(g, "LD")
                g.dma_start(out=d_y[0:128, :], in_=gb[:, 0:2048]).then_inc(sems["GD"], 16)
                g.dma_start(out=d_y[128:256, :], in_=xcb[:, 0:2048]).then_inc(sems["GD"], 16)
                g.dma_start(out=d_y[256:384, :], in_=xib[:, 0:2048]).then_inc(sems["GD"], 16)
                g.dma_start(out=d_y[448:464, 0:512], in_=bcb[:]).then_inc(sems["GD"], 16)
                g.dma_start(out=d_y[384:448, 0:256], in_=dtbuf[:]).then_inc(sems["GD"], 16)
                g.dma_start(out=d_y[896:1024, :], in_=xnb[:, 0:2048]).then_inc(sems["GD"], 16)
                cnt["GD"] += 16 * 6
                wait(g, "GD")

    if upto == "p3dbg":
        ctx.close()
        return nc

    if upto == "p3":
        tap_xafter_and_finish()
        ctx.close()
        return nc

    with ExitStack() as p4:
        pr = p4.enter_context
        X4 = pr(nc.sbuf_tensor("X4", [128, ND * T], F32, side="right"))

        @blk.sync
        def _(q):
            wait(q, "GD")
            for j in range(ND):
                q.dma_start(out=X4[:, j * T:(j + 1) * T],
                            in_=d_xafter[j * 128:(j + 1) * 128, :]
                            ).then_inc(sems["LD"], 16)
            cnt["LD"] += 16 * ND
        barrier()

        with ExitStack() as lnscope:
            lr = lnscope.enter_context
            scr = (lr(nc.sbuf_tensor("g_xb", [128, ND * 512], BF16, side="right")),
                   lr(nc.sbuf_tensor("g_sq", [128, ND * 512], BF16, side="right")),
                   lr(nc.sbuf_tensor("g_m", [128, 512], F32, side="right")),
                   lr(nc.sbuf_tensor("g_r", [128, 512], F32, side="right")),
                   lr(nc.sbuf_tensor("g_t", [128, 512], F32, side="right")))
            emit_ln(lambda j, cs: X4[:, j * T:(j + 1) * T][:, cs], scr)
            barrier()

        bufs = (pr(nc.sbuf_tensor("g_h1", [128, 32 * TCF], BF16, side="right")),
                pr(nc.sbuf_tensor("g_lr", [16, TCF], BF16, side="right")),
                pr(nc.sbuf_tensor("g_lr2", [16, TCF], BF16, side="right")),
                pr(nc.sbuf_tensor("g_wfc", [128, 2 * 1024], BF16, side="right")),
                pr(nc.sbuf_tensor("g_wpj", [128, 2 * 4096], BF16, side="right")))
        ffn_chunks(X4, bufs)

        @blk.gpsimd
        def _(g):
            for nm in ("T", "S", "V", "LD", "SD"):
                g.wait_ge(sems[nm], cnt[nm])
            for j in range(ND):
                g.dma_start(out=d_y[j * 128:(j + 1) * 128, :],
                            in_=X4[:, j * T:(j + 1) * T]).then_inc(sems["GD"], 16)
            cnt["GD"] += 16 * ND
            wait(g, "GD")

    ctx.close()
    return nc


_CACHE = {}
_UPTO = ["full"]


def kernel(**inputs):
    x = np.asarray(inputs["L0_attn_a_x"])
    B = x.shape[0]

    def col(v, n):
        return np.ascontiguousarray(np.asarray(v).reshape(n, 128).T)

    shared = {
        "wz": np.ascontiguousarray(inputs["L0_attn_a_Wz"]),
        "wh": np.ascontiguousarray(inputs["L0_attn_a_Wh"]),
        "bz": col(inputs["L0_attn_a_bz"], ND),
        "bh": col(inputs["L0_attn_a_bh"], ND),
        "logd": col(inputs["L0_attn_a_log_D"], ND),
        "wfcT": np.ascontiguousarray(np.asarray(inputs["L0_ffn_W_fc"]).T),
        "wprojT": np.ascontiguousarray(np.asarray(inputs["L0_ffn_W_proj"]).T),
        "afcT": np.ascontiguousarray(np.asarray(inputs["L0_ffn_A_fc"]).T),
        "bfcT": np.ascontiguousarray(np.asarray(inputs["L0_ffn_B_fc"]).T),
        "aprojT": np.ascontiguousarray(np.asarray(inputs["L0_ffn_A_proj"]).T),
        "bprojT": np.ascontiguousarray(np.asarray(inputs["L0_ffn_B_proj"]).T),
        "bfc": col(inputs["L0_ffn_b_fc"], 32),
        "bproj": col(inputs["L0_ffn_b_proj"], ND),
    }
    for l in range(NLAYER):
        shared[f"inwT{l}"] = np.ascontiguousarray(
            np.asarray(inputs["L0_attn_b_in_proj_w"])[l].T)
        shared[f"xpwT{l}"] = np.ascontiguousarray(
            np.asarray(inputs["L0_attn_b_x_proj_w"])[l].T)
        shared[f"dtwT{l}"] = np.ascontiguousarray(
            np.asarray(inputs["L0_attn_b_dt_proj_w"])[l].T)
        shared[f"opwT{l}"] = np.ascontiguousarray(
            np.asarray(inputs["L0_attn_b_out_proj_w"])[l].T)
        cw = np.asarray(inputs["L0_attn_b_conv_w"])[l]
        shared[f"cw{l}"] = np.ascontiguousarray(
            cw.reshape(NDI, 128, 4).transpose(1, 0, 2).reshape(128, NDI * 4))
        shared[f"cb{l}"] = col(np.asarray(inputs["L0_attn_b_conv_b"])[l], NDI)
        shared[f"dtb{l}"] = col(np.asarray(inputs["L0_attn_b_dt_proj_b"])[l], NDI)
        al = np.asarray(inputs["L0_attn_b_A_log"])[l]
        shared[f"alog{l}"] = np.ascontiguousarray(
            al.reshape(NDI, 128, NS).transpose(1, 0, 2).reshape(128, NDI * NS))
        shared[f"dsk{l}"] = col(np.asarray(inputs["L0_attn_b_D"])[l], NDI)

    key = _UPTO[0]
    if key not in _CACHE:
        _CACHE[key] = build(key)
    nc = _CACHE[key]

    in_maps = []
    for i in range(B):
        m = {"x": np.ascontiguousarray(x[i].T)}
        m.update(shared)
        in_maps.append(m)
    res = run_bass_kernel_spmd(nc, in_maps, core_ids=list(range(B)))
    out = np.stack([np.asarray(r["y"]).T for r in res.results], axis=0)
    return out.astype(np.float32)


# version 14 (final)
# speedup vs baseline: 1.0321x; 1.0274x over previous
from contextlib import ExitStack

import numpy as np

import concourse.bass as bass
import concourse.mybir as mybir
from concourse.bass_isa import ReduceOp
from concourse.bass_utils import run_bass_kernel_spmd

F32 = mybir.dt.float32
BF16 = mybir.dt.bfloat16
AF = mybir.ActivationFunctionType
ALU = mybir.AluOpType
AX = mybir.AxisListType

D = 1024
T = 2048
DI = 2048
NS = 16
ND = D // 128
NDI = DI // 128
FF = 4096
TC = 256
NCH = T // TC
TCA = 256
TCF = 512
DCOLS = 256
SG = 4
MAGIC = 12582912.0
NLAYER = 4
_NL = [4]
DBGNOCARRY = False
DBGNOLORA = False
DBGNOQUANT = False
DBGQUANTP1ONLY = False
DBGQUANTSTAGE = 0

C_BZ = 0
C_BH = 8
C_DCOEF = 24
C_BFC = 32
C_BPJ = 64
C_EPS = 72
C_ONE = 73
C_IFC = 128
C_SFC = 136
C_IPJ = 160
C_SPJ = 168
C_127 = 184
C_R127 = 185
C_INVD = 79
C_CB = 80
C_DTB = 96
C_DSK = 112
C_NEG1 = 176
C_LOGD = 144


def build(upto="full"):
    nc = bass.Bass("TRN2", target_bir_lowering=False, debug=False)

    def di(name, shape, dt=F32):
        return nc.dram_tensor(name, shape, dt, kind="ExternalInput").ap()

    def dint(name, shape, dt):
        return nc.dram_tensor(name, shape, dt, kind="Internal").ap()

    d_x = di("x", [D, T])
    d_wz = di("wz", [D, D])
    d_wh = di("wh", [D, D])
    d_bz = di("bz", [128, ND])
    d_bh = di("bh", [128, ND])
    d_logd = di("logd", [128, ND])
    d_wfcT = di("wfcT", [D, FF])
    d_wprojT = di("wprojT", [FF, D])
    d_afcT = di("afcT", [D, 16])
    d_bfcT = di("bfcT", [16, FF])
    d_aprojT = di("aprojT", [FF, 16])
    d_bprojT = di("bprojT", [16, D])
    d_bfc = di("bfc", [128, 32])
    d_bproj = di("bproj", [128, ND])
    d_inwT = [di(f"inwT{l}", [D, 2 * DI]) for l in range(NLAYER)]
    d_xpwT = [di(f"xpwT{l}", [DI, 96]) for l in range(NLAYER)]
    d_dtwT = [di(f"dtwT{l}", [64, DI]) for l in range(NLAYER)]
    d_opwT = [di(f"opwT{l}", [DI, D]) for l in range(NLAYER)]
    d_cw = [di(f"cw{l}", [128, NDI * 4]) for l in range(NLAYER)]
    d_cb = [di(f"cb{l}", [128, NDI]) for l in range(NLAYER)]
    d_dtb = [di(f"dtb{l}", [128, NDI]) for l in range(NLAYER)]
    d_alog = [di(f"alog{l}", [128, NDI * NS]) for l in range(NLAYER)]
    d_dsk = [di(f"dsk{l}", [128, NDI]) for l in range(NLAYER)]

    d_y = nc.dram_tensor("y", [D, T], F32, kind="ExternalOutput").ap()

    d_xspill = dint("xspill", [D, T], F32)
    d_xafter = dint("xafter", [D, T], F32)
    d_wqfcB = dint("wqfcB", [32 * 128, 8 * 128], BF16)
    d_wqpjB = dint("wqpjB", [8 * 128, 32 * 128], BF16)
    d_inwB = [dint(f"inwB{l}", [32 * 128, 8 * 128], BF16) for l in range(NLAYER)]
    d_opwB = [dint(f"opwB{l}", [8 * 128, 16 * 128], BF16) for l in range(NLAYER)]
    d_xpwB = [dint(f"xpwB{l}", [DI, 96], BF16) for l in range(NLAYER)]
    d_dtwB = [dint(f"dtwB{l}", [64, DI], BF16) for l in range(NLAYER)]

    ctx = ExitStack()
    ec = ctx.enter_context

    sems = {}
    for nm in ("LD", "GD", "SD", "T", "S", "V"):
        sems[nm] = ec(nc.semaphore(nm))
    cnt = {nm: 0 for nm in sems}

    def inc(instr, nm, n=1):
        instr.then_inc(sems[nm], n)
        cnt[nm] += n
        return cnt[nm]

    def wait(eng, nm, val=None):
        eng.wait_ge(sems[nm], cnt[nm] if val is None else val)

    xnb = ec(nc.sbuf_tensor("xnb", [128, ND * T], BF16))
    colsF = ec(nc.sbuf_tensor("colsF", [128, 192], F32))
    acolF = ec(nc.sbuf_tensor("acolF", [128, NDI * NS], F32))
    cwF = ec(nc.sbuf_tensor("cwF", [128, NDI * 4], F32))
    selB = ec(nc.sbuf_tensor("selB", [16, 16 * 128], BF16))
    onesB = ec(nc.sbuf_tensor("onesB", [128, 128], BF16))
    afcB = ec(nc.sbuf_tensor("afcB", [128, ND * 16], BF16))
    bfc2B = ec(nc.sbuf_tensor("bfc2B", [16, FF], BF16))
    aprojB = ec(nc.sbuf_tensor("aprojB", [128, 32 * 16], BF16))
    bproj2B = ec(nc.sbuf_tensor("bproj2B", [16, D], BF16))
    mcol = ec(nc.sbuf_tensor("mcol", [128, 48], F32))
    qsc = ec(nc.sbuf_tensor("qsc", [128, 512], F32))

    x1_stack = ExitStack()
    X1 = x1_stack.enter_context(nc.sbuf_tensor("X1", [128, ND * T], F32))

    psA = ec(nc.psum_tensor("psA", [128, 512], F32))
    psB = ec(nc.psum_tensor("psB", [128, 512], F32))
    psC = ec(nc.psum_tensor("psC", [128, 512], F32))
    psD = ec(nc.psum_tensor("psD", [128, 512], F32))
    psS96 = ec(nc.psum_tensor("psS96", [96, 512], F32))
    psL16 = ec(nc.psum_tensor("psL16", [16, 512], F32))
    psL16b = ec(nc.psum_tensor("psL16b", [16, 512], F32))
    psX = ec(nc.psum_tensor("psX", [64, 64], F32))

    blk = ec(nc.Block())

    def tdrain(tE):
        return tE.matmul(psX[:, 0:64], onesB[:, 0:64], onesB[:, 64:128],
                         start=True, stop=True)

    def barrier():
        tgt = dict(cnt)
        for dec, own in ((blk.tensor, "T"), (blk.scalar, "S"), (blk.vector, "V"),
                         (blk.sync, None), (blk.gpsimd, None)):
            def mk(own=own):
                def f(e):
                    for nm in ("T", "S", "V", "LD", "GD", "SD"):
                        if nm != own:
                            e.wait_ge(sems[nm], tgt[nm])
                return f
            dec(mk())

    def mm_stage(items, banks, evac_eng="S", evac_prewait=(), entry=None):
        nb = len(banks)
        t0 = cnt["T"]
        e0 = cnt[evac_eng]
        ld0 = cnt["LD"]
        nload = [0] * len(items)
        if any(it.get("load") for it in items):
            @blk.sync
            def _(q):
                for i, it in enumerate(items):
                    if not it.get("load"):
                        continue
                    if i >= 2:
                        q.wait_ge(sems["T"], t0 + i - 1)
                    nload[i] = it["load"](q)
                    cnt["LD"] += 16 * nload[i]
        cum = ld0
        cumload = []
        for n in nload:
            cum += 16 * n
            cumload.append(cum)

        sS0 = cnt["S"] if entry is None else entry[0]
        sV0 = cnt["V"] if entry is None else entry[1]

        @blk.tensor
        def _(tE):
            tE.wait_ge(sems["S"], sS0)
            tE.wait_ge(sems["V"], sV0)
            for i, it in enumerate(items):
                if nload[i]:
                    tE.wait_ge(sems["LD"], cumload[i])
                if i >= nb:
                    tE.wait_ge(sems[evac_eng], e0 + i - nb + 1)
                last = it["mms"](tE, banks[i % nb])
                last.then_inc(sems["T"], 1)
                cnt["T"] += 1
            last = tdrain(tE)
            last.then_inc(sems["T"], 1)
            cnt["T"] += 1

        dec = blk.scalar if evac_eng == "S" else blk.vector

        @dec
        def _(eE):
            for nm, tgt in evac_prewait:
                eE.wait_ge(sems[nm], tgt)
            for i, it in enumerate(items):
                eE.wait_ge(sems["T"], t0 + i + 2)
                last = it["evac"](eE, banks[i % nb])
                last.then_inc(sems[evac_eng], 1)
                cnt[evac_eng] += 1

    def emit_ln(src_ap, scr):
        xb, sqb, mb, rb, tb = scr
        for c0 in range(0, T, 512):
            cs = slice(c0, c0 + 512)

            @blk.scalar
            def _(s):
                wait(s, "V")
                wait(s, "T")
                for j in range(ND):
                    s.copy(xb[:, j * 512:(j + 1) * 512], src_ap(j, cs))
                for j in range(ND):
                    last = s.activation(sqb[:, j * 512:(j + 1) * 512],
                                        xb[:, j * 512:(j + 1) * 512], AF.Square)
                inc(last, "S")

            @blk.tensor
            def _(tE):
                wait(tE, "S")
                for j in range(ND):
                    tE.matmul(psA[:], onesB[:], xb[:, j * 512:(j + 1) * 512],
                              start=(j == 0), stop=(j == ND - 1))
                for j in range(ND):
                    tE.matmul(psB[:], onesB[:], sqb[:, j * 512:(j + 1) * 512],
                              start=(j == 0), stop=(j == ND - 1))
                inc(tdrain(tE), "T")

            @blk.scalar
            def _(s):
                wait(s, "T")
                last = s.activation(mb[:], psA[:], AF.Copy,
                                    scale=colsF[:, C_INVD:C_INVD + 1])
                inc(last, "S")

            @blk.vector
            def _(v):
                wait(v, "S")
                v.tensor_tensor(tb[:], mb[:], mb[:], ALU.mult)
                last = v.scalar_tensor_tensor(tb[:], psB[:],
                                              colsF[:, C_INVD:C_INVD + 1], tb[:],
                                              ALU.mult, ALU.subtract)
                inc(last, "V")

            @blk.scalar
            def _(s):
                wait(s, "V")
                last = s.activation(rb[:], tb[:], AF.Sqrt,
                                    bias=colsF[:, C_EPS:C_EPS + 1])
                inc(last, "S")

            @blk.vector
            def _(v):
                wait(v, "S")
                v.reciprocal(rb[:], rb[:])
                for j in range(ND):
                    v.scalar_tensor_tensor(tb[:], mb[:], -1.0, src_ap(j, cs),
                                           ALU.mult, ALU.add)
                    last = v.tensor_tensor(xnb[:, j * T + c0:j * T + c0 + 512],
                                           tb[:], rb[:], ALU.mult)
                inc(last, "V")

    with ExitStack() as p0:
        pr = p0.enter_context
        stgF = pr(nc.sbuf_tensor("stgF", [128, 2 * FF], F32, side="right"))
        stgB = pr(nc.sbuf_tensor("stgB", [128, 2 * FF], BF16, side="right"))
        qb = pr(nc.sbuf_tensor("qb", [128, FF], F32, side="right"))
        qbf = pr(nc.sbuf_tensor("qbf", [128, FF], BF16, side="right"))

        @blk.sync
        def _(q):
            for j in range(ND):
                q.dma_start(out=X1[:, j * T:(j + 1) * T],
                            in_=d_x[j * 128:(j + 1) * 128, :]).then_inc(sems["LD"], 16)
            q.dma_start(out=colsF[:, C_BZ:C_BZ + 8], in_=d_bz).then_inc(sems["LD"], 16)
            q.dma_start(out=colsF[:, C_BH:C_BH + 8], in_=d_bh).then_inc(sems["LD"], 16)
            q.dma_start(out=colsF[:, C_LOGD:C_LOGD + 8], in_=d_logd).then_inc(sems["LD"], 16)
            q.dma_start(out=colsF[:, C_BFC:C_BFC + 32], in_=d_bfc).then_inc(sems["LD"], 16)
            q.dma_start(out=colsF[:, C_BPJ:C_BPJ + 8], in_=d_bproj).then_inc(sems["LD"], 16)
            cnt["LD"] += 16 * (ND + 5)

        @blk.gpsimd
        def _(g):
            for j in range(ND):
                g.dma_start(out=afcB[:, j * 16:(j + 1) * 16],
                            in_=d_afcT[j * 128:(j + 1) * 128, :]).then_inc(sems["GD"], 16)
            for j in range(32):
                g.dma_start(out=aprojB[:, j * 16:(j + 1) * 16],
                            in_=d_aprojT[j * 128:(j + 1) * 128, :]).then_inc(sems["GD"], 16)
            cnt["GD"] += 16 * (ND + 32)

        @blk.vector
        def _(v):
            v.memset(colsF[:, C_EPS:C_EPS + 1], 1e-5)
            v.memset(colsF[:, C_ONE:C_ONE + 1], 1.0)
            v.memset(colsF[:, C_INVD:C_INVD + 1], 1.0 / D)
            v.memset(colsF[:, C_NEG1:C_NEG1 + 1], -1.0)
            v.memset(colsF[:, C_127:C_127 + 1], 127.0)
            v.memset(colsF[:, C_R127:C_R127 + 1], 1.0 / 127.0)
            v.memset(onesB[:], 1.0)
            v.memset(selB[:], 0.0)
            wait(v, "LD")
            last = v.tensor_copy(colsF[:, C_DCOEF:C_DCOEF + 8],
                                 colsF[:, C_LOGD:C_LOGD + 8])
            inc(last, "V")

        @blk.scalar
        def _(s):
            wait(s, "V")
            last = s.activation(colsF[:, C_DCOEF:C_DCOEF + 8],
                                colsF[:, C_DCOEF:C_DCOEF + 8], AF.Exp)
            inc(last, "S")

        @blk.sync
        def _(q):
            wait(q, "V")
            for s_ in range(16):
                q.dma_start(out=selB[s_:s_ + 1, s_ * 128:(s_ + 1) * 128],
                            in_=onesB[0:1, 0:128]).then_inc(sems["LD"], 16)
            cnt["LD"] += 16 * 16

        @blk.sync
        def _(q):
            q.dma_start(out=stgF[:16, 0:FF], in_=d_bfcT).then_inc(sems["LD"], 16)
            q.dma_start(out=stgF[:16, FF:FF + D], in_=d_bprojT).then_inc(sems["LD"], 16)
            cnt["LD"] += 32

        @blk.vector
        def _(v):
            wait(v, "LD")
            v.tensor_scalar(bfc2B[:], stgF[:16, 0:FF], 2.0, None, ALU.mult)
            last = v.tensor_scalar(bproj2B[:], stgF[:16, FF:FF + D], 2.0,
                                   None, ALU.mult)
            inc(last, "V")
        barrier()

        def quant(src, nptile, nslab, blkdst, qcol):
            w = nslab * 128
            ventry = cnt["V"]
            ld0 = cnt["LD"]
            v0 = cnt["V"]

            @blk.sync
            def _(q):
                for k in range(nptile):
                    if k < 2:
                        q.wait_ge(sems["V"], ventry)
                    else:
                        q.wait_ge(sems["V"], v0 + k - 1)
                    q.dma_start(out=stgF[:, (k % 2) * w:(k % 2) * w + w],
                                in_=src[k * 128:(k + 1) * 128, :]).then_inc(sems["LD"], 16)
                cnt["LD"] += 16 * nptile

            @blk.vector
            def _(v):
                for k in range(nptile):
                    v.wait_ge(sems["LD"], ld0 + 16 * (k + 1))
                    last = v.tensor_reduce(mcol[:, k:k + 1],
                                           stgF[:, (k % 2) * w:(k % 2) * w + w],
                                           AX.X, ALU.max,
                                           apply_absolute_value=True)
                    last.then_inc(sems["V"], 1)
                    cnt["V"] += 1
                    v.wait_ge(sems["V"], cnt["V"])
                last = v.tensor_reduce(mcol[:, 32:33], mcol[:, 0:nptile],
                                       AX.X, ALU.max)
                inc(last, "V")
                wait(v, "V")

            @blk.sync
            def _(q):
                wait(q, "V")
                q.dma_start(out=qb[0:1, 0:128],
                            in_=mcol[:, 32:33]).then_inc(sems["LD"], 16)
                cnt["LD"] += 16

            @blk.vector
            def _(v):
                wait(v, "LD")
                v.memset(qb[0:1, 256:384], 1.0)
                last = v.tensor_reduce(qb[0:1, 128:129], qb[0:1, 0:128],
                                       AX.X, ALU.max)
                inc(last, "V")
                wait(v, "V")
                bsc = qb[0:1, 128:129].broadcast_to([1, 128])
                last = v.tensor_tensor(qb[0:1, 384:512], bsc, bsc, ALU.max)
                inc(last, "V")

            @blk.tensor
            def _(tE):
                wait(tE, "V")
                tE.matmul(psA[:, 0:128], qb[0:1, 256:384],
                          qb[0:1, 384:512], start=True, stop=True)
                inc(tdrain(tE), "T")

            @blk.vector
            def _(v):
                wait(v, "T")
                negb = colsF[:, C_NEG1:C_NEG1 + 1].broadcast_to([128, 128])
                q0 = qcol
                v.tensor_tensor(qsc[:, q0:q0 + 128], psA[:, 0:128], negb,
                                ALU.max)
                v.reciprocal(qsc[:, q0 + 128:q0 + 256], qsc[:, q0:q0 + 128])
                v.tensor_tensor(
                    qsc[:, q0 + 128:q0 + 256], qsc[:, q0 + 128:q0 + 256],
                    colsF[:, C_127:C_127 + 1].broadcast_to([128, 128]),
                    ALU.mult)
                last = v.tensor_tensor(
                    qsc[:, q0:q0 + 128], qsc[:, q0:q0 + 128],
                    colsF[:, C_R127:C_R127 + 1].broadcast_to([128, 128]),
                    ALU.mult)
                inc(last, "V")
                wait(v, "V")

            if DBGQUANTP1ONLY:
                return
            ld1 = cnt["LD"]
            v1 = cnt["V"]
            sd0 = cnt["SD"]

            @blk.sync
            def _(q):
                for k in range(nptile):
                    if k < 2:
                        q.wait_ge(sems["V"], v1)
                    else:
                        q.wait_ge(sems["V"], v1 + 3 * (k - 1))
                    q.dma_start(out=stgF[:, (k % 2) * w:(k % 2) * w + w],
                                in_=src[k * 128:(k + 1) * 128, :]).then_inc(sems["LD"], 16)
                cnt["LD"] += 16 * nptile

            @blk.vector
            def _(v):
                v.wait_ge(sems["V"], v1)
                for k in range(nptile):
                    v.wait_ge(sems["LD"], ld1 + 16 * (k + 1))
                    if k >= 1:
                        v.wait_ge(sems["SD"], sd0 + 16 * nslab * k)
                    sl = stgF[:, (k % 2) * w:(k % 2) * w + w]
                    v.tensor_scalar(qb[:, 0:w], sl,
                                    qsc[:, qcol + 128:qcol + 129],
                                    MAGIC, ALU.mult, ALU.add)
                    v.tensor_scalar(qb[:, 0:w], qb[:, 0:w], MAGIC, -128.0,
                                    ALU.subtract, ALU.max)
                    last = v.tensor_scalar(qbf[:, 0:w], qb[:, 0:w], 127.0,
                                           qsc[:, qcol:qcol + 1],
                                           ALU.min, ALU.mult)
                    last.then_inc(sems["V"], 3)
                    cnt["V"] += 3

            @blk.scalar
            def _(s):
                for k in range(nptile):
                    s.wait_ge(sems["V"], v1 + 3 * (k + 1))
                    for m in range(nslab):
                        s.dma_start(out=blkdst[m * 128:(m + 1) * 128,
                                               k * 128:(k + 1) * 128],
                                    in_=qbf[:, m * 128:(m + 1) * 128]
                                    ).then_inc(sems["SD"], 16)
                    cnt["SD"] += 16 * nslab

        if not DBGNOQUANT:
            quant(d_wfcT, 8, 32, d_wqfcB, 0)
            quant(d_wprojT, 32, 8, d_wqpjB, 256)

        jobs = []
        for l in range(NLAYER):
            for k in range(8):
                jobs.append((d_inwT[l][k * 128:(k + 1) * 128, :], 128, 4096,
                             [(d_inwB[l][m * 128:(m + 1) * 128,
                                         k * 128:(k + 1) * 128], m * 128)
                              for m in range(32)]))
            for k in range(16):
                jobs.append((d_opwT[l][k * 128:(k + 1) * 128, :], 128, 1024,
                             [(d_opwB[l][m * 128:(m + 1) * 128,
                                         k * 128:(k + 1) * 128], m * 128)
                              for m in range(8)]))
            for k in range(16):
                jobs.append((d_xpwT[l][k * 128:(k + 1) * 128, :], 128, 96,
                             [(d_xpwB[l][k * 128:(k + 1) * 128, :], 0)]))
            jobs.append((d_dtwT[l], 64, 2048, [(d_dtwB[l], 0)]))

        gd0 = cnt["GD"]
        sd0 = cnt["SD"]
        cum_sd = []
        acc = sd0
        for (_, _, _, outs) in jobs:
            acc += 16 * len(outs)
            cum_sd.append(acc)

        @blk.gpsimd
        def _(g):
            for i, (srcap, p, w, outs) in enumerate(jobs):
                if i >= 2:
                    g.wait_ge(sems["SD"], cum_sd[i - 2])
                g.dma_start(out=stgB[0:p, (i % 2) * FF:(i % 2) * FF + w],
                            in_=srcap).then_inc(sems["GD"], 16)
                cnt["GD"] += 16

        @blk.scalar
        def _(s):
            for i, (srcap, p, w, outs) in enumerate(jobs):
                s.wait_ge(sems["GD"], gd0 + 16 * (i + 1))
                for dst, c0 in outs:
                    ww = dst.shape[1]
                    s.dma_start(out=dst,
                                in_=stgB[0:p, (i % 2) * FF + c0:
                                         (i % 2) * FF + c0 + ww]
                                ).then_inc(sems["SD"], 16)
                cnt["SD"] += 16 * len(outs)
        barrier()

    if upto == "p0":
        @blk.sync
        def _(q):
            for nm in ("T", "S", "V", "GD", "SD"):
                q.wait_ge(sems[nm], cnt[nm])
            q.dma_start(out=d_y[0:128, 0:192], in_=colsF[:]).then_inc(sems["LD"], 16)
            cnt["LD"] += 16
            wait(q, "LD")

        @blk.gpsimd
        def _(g):
            wait(g, "LD")
            g.dma_start(out=d_y[0:128, 192:1216],
                        in_=d_wqfcB[0:128, :]).then_inc(sems["GD"], 16)
            g.dma_start(out=d_y[0:128, 1216:1312],
                        in_=d_xpwB[0][0:128, :]).then_inc(sems["GD"], 16)
            g.dma_start(out=d_y[128:256, 0:1024],
                        in_=d_inwB[0][0:128, :]).then_inc(sems["GD"], 16)
            g.dma_start(out=d_y[256:384, 0:2048],
                        in_=d_opwB[0][0:128, :]).then_inc(sems["GD"], 16)
            cnt["GD"] += 64
            wait(g, "GD")
        x1_stack.close()
        ctx.close()
        return nc

    with ExitStack() as p1:
        pr = p1.enter_context
        wzB = pr(nc.sbuf_tensor("wzB", [128, ND * D], BF16, side="right"))
        whB = pr(nc.sbuf_tensor("whB", [128, ND * D], BF16, side="right"))

        @blk.gpsimd
        def _(g):
            for j in range(ND):
                g.dma_start(out=wzB[:, j * D:(j + 1) * D],
                            in_=d_wz[j * 128:(j + 1) * 128, :]).then_inc(sems["GD"], 16)
                g.dma_start(out=whB[:, j * D:(j + 1) * D],
                            in_=d_wh[j * 128:(j + 1) * 128, :]).then_inc(sems["GD"], 16)
            cnt["GD"] += 16 * 2 * ND

        with ExitStack() as lnscope:
            lr = lnscope.enter_context
            scr = (lr(nc.sbuf_tensor("a_xb", [128, ND * 512], BF16, side="right")),
                   lr(nc.sbuf_tensor("a_sq", [128, ND * 512], BF16, side="right")),
                   lr(nc.sbuf_tensor("a_m", [128, 512], F32, side="right")),
                   lr(nc.sbuf_tensor("a_r", [128, 512], F32, side="right")),
                   lr(nc.sbuf_tensor("a_t", [128, 512], F32, side="right")))
            emit_ln(lambda j, cs: X1[:, j * T:(j + 1) * T][:, cs], scr)
            barrier()

        HH = 4
        TCH = TCA + HH
        zb = pr(nc.sbuf_tensor("zb", [128, ND * TCA], F32, side="right"))
        htb = pr(nc.sbuf_tensor("htb", [128, ND * TCA], F32, side="right"))
        asb = pr(nc.sbuf_tensor("asb", [128, ND * TCH], F32, side="right"))
        bsb = pr(nc.sbuf_tensor("bsb", [128, ND * TCH], F32, side="right"))
        hcc = pr(nc.sbuf_tensor("hcc", [128, ND * TCH], F32, side="right"))
        hlb = pr(nc.sbuf_tensor("hlb", [128, ND * DCOLS], F32, side="right"))
        hrb = pr(nc.sbuf_tensor("hrb", [128, ND * DCOLS], F32, side="right"))
        dfb = pr(nc.sbuf_tensor("dfb", [128, DCOLS], F32, side="right"))

        @blk.vector
        def _(v):
            v.memset(asb[:], 0.0)
            last = v.memset(bsb[:], 0.0)
            inc(last, "V")

        for cc in range(T // TCA):
            cs = slice(cc * TCA, (cc + 1) * TCA)
            items = []
            for j in range(ND):
                def mk_mm(j=j, cs=cs):
                    def f(tE, ps):
                        for k in range(ND):
                            last = tE.matmul(
                                ps[:, 0:TCA],
                                wzB[:, k * D + j * 128:k * D + (j + 1) * 128],
                                xnb[:, k * T:(k + 1) * T][:, cs],
                                start=(k == 0), stop=(k == ND - 1))
                        return last
                    return f

                def mk_ev(j=j):
                    def f(sE, ps):
                        return sE.activation(zb[:, j * TCA:(j + 1) * TCA],
                                             ps[:, 0:TCA], AF.Sigmoid,
                                             bias=colsF[:, C_BZ + j:C_BZ + j + 1])
                    return f
                items.append({"mms": mk_mm(), "evac": mk_ev()})
            for j in range(ND):
                def mk_mm(j=j, cs=cs):
                    def f(tE, ps):
                        for k in range(ND):
                            last = tE.matmul(
                                ps[:, 0:TCA],
                                whB[:, k * D + j * 128:k * D + (j + 1) * 128],
                                xnb[:, k * T:(k + 1) * T][:, cs],
                                start=(k == 0), stop=(k == ND - 1))
                        return last
                    return f

                def mk_ev(j=j):
                    def f(sE, ps):
                        return sE.activation(htb[:, j * TCA:(j + 1) * TCA],
                                             ps[:, 0:TCA], AF.Identity,
                                             bias=colsF[:, C_BH + j:C_BH + j + 1])
                    return f
                items.append({"mms": mk_mm(), "evac": mk_ev()})
            if cc == 0:
                @blk.tensor
                def _(tE):
                    wait(tE, "GD")
            mm_stage(items, [psA, psB])
            barrier()

            @blk.vector
            def _(v):
                for j in range(ND):
                    sj = slice(j * TCA, (j + 1) * TCA)
                    b0 = j * TCH
                    v.tensor_scalar(asb[:, b0 + HH:b0 + TCH], zb[:, sj],
                                    -1.0, 1.0, ALU.mult, ALU.add)
                    last = v.tensor_tensor(bsb[:, b0 + HH:b0 + TCH], zb[:, sj],
                                           htb[:, sj], ALU.mult)
                    inc(last, "V")
                    wait(v, "V")
                    last = v.tensor_tensor_scan(hcc[:, b0:b0 + TCH],
                                                asb[:, b0:b0 + TCH],
                                                bsb[:, b0:b0 + TCH],
                                                0.0, ALU.mult, ALU.add)
                    inc(last, "V")
                    wait(v, "V")
                    lc = hcc[:, b0 + TCH - HH:b0 + TCH]
                    last = v.tensor_tensor(bsb[:, b0:b0 + HH], lc, lc, ALU.max)
                inc(last, "V")
            barrier()

            for d0 in range(0, TCA, DCOLS):
                @blk.sync
                def _(q):
                    wait(q, "V")
                    for j in range(ND):
                        sj = slice(j * TCH + HH + d0, j * TCH + HH + d0 + DCOLS)
                        jm, jp = (j - 1) % ND, (j + 1) % ND
                        sjm = slice(jm * TCH + HH + d0, jm * TCH + HH + d0 + DCOLS)
                        sjp = slice(jp * TCH + HH + d0, jp * TCH + HH + d0 + DCOLS)
                        dj = slice(j * DCOLS, (j + 1) * DCOLS)
                        q.dma_start(out=hlb[:, dj][1:128, :],
                                    in_=hcc[:, sj][0:127, :]).then_inc(sems["LD"], 16)
                        q.dma_start(out=hlb[:, dj][0:1, :],
                                    in_=hcc[:, sjm][127:128, :]).then_inc(sems["LD"], 16)
                        q.dma_start(out=hrb[:, dj][0:127, :],
                                    in_=hcc[:, sj][1:128, :]).then_inc(sems["LD"], 16)
                        q.dma_start(out=hrb[:, dj][127:128, :],
                                    in_=hcc[:, sjp][0:1, :]).then_inc(sems["LD"], 16)
                    cnt["LD"] += 16 * 4 * ND

                @blk.vector
                def _(v):
                    wait(v, "LD")
                    for j in range(ND):
                        sj = slice(j * TCH + HH + d0, j * TCH + HH + d0 + DCOLS)
                        dj = slice(j * DCOLS, (j + 1) * DCOLS)
                        v.tensor_tensor(dfb[:], hlb[:, dj], hrb[:, dj], ALU.add)
                        v.scalar_tensor_tensor(dfb[:], hcc[:, sj], -2.0, dfb[:],
                                               ALU.mult, ALU.add)
                        v.scalar_tensor_tensor(
                            dfb[:], dfb[:], colsF[:, C_DCOEF + j:C_DCOEF + j + 1],
                            hcc[:, sj], ALU.mult, ALU.add)
                        xj = X1[:, j * T:(j + 1) * T][:, cc * TCA + d0:
                                                      cc * TCA + d0 + DCOLS]
                        last = v.tensor_tensor(xj, xj, dfb[:], ALU.add)
                    inc(last, "V")
                barrier()
        if upto == "p1dbg":
            @blk.sync
            def _(q):
                for nm in ("T", "S", "V", "GD", "SD"):
                    q.wait_ge(sems[nm], cnt[nm])
                q.dma_start(out=d_y[0:128, :], in_=hcc[:, 0:2048]).then_inc(sems["LD"], 16)
                q.dma_start(out=d_y[128:256, :], in_=asb[:, 0:2048]).then_inc(sems["LD"], 16)
                q.dma_start(out=d_y[256:384, :], in_=bsb[:, 0:2048]).then_inc(sems["LD"], 16)
                q.dma_start(out=d_y[384:512, :], in_=zb[:, 0:2048]).then_inc(sems["LD"], 16)
                q.dma_start(out=d_y[512:640, :], in_=htb[:, 0:2048]).then_inc(sems["LD"], 16)
                q.dma_start(out=d_y[768:896, :], in_=X1[:, 0:2048]).then_inc(sems["LD"], 16)
                cnt["LD"] += 16 * 6
                wait(q, "LD")

            @blk.gpsimd
            def _(g):
                wait(g, "LD")
                g.dma_start(out=d_y[640:768, :], in_=xnb[:, 0:2048]).then_inc(sems["GD"], 16)
                cnt["GD"] += 16
                wait(g, "GD")

    def tap_X1_and_finish():
        @blk.sync
        def _(q):
            for nm in ("T", "S", "V", "GD", "SD"):
                q.wait_ge(sems[nm], cnt[nm])
            for j in range(ND):
                q.dma_start(out=d_y[j * 128:(j + 1) * 128, :],
                            in_=X1[:, j * T:(j + 1) * T]).then_inc(sems["LD"], 16)
            cnt["LD"] += 16 * ND
            wait(q, "LD")

    def tap_xafter_and_finish():
        @blk.sync
        def _(q):
            for nm in ("T", "S", "V", "GD", "SD"):
                q.wait_ge(sems[nm], cnt[nm])
            q.dma_start(out=d_y, in_=d_xafter).then_inc(sems["LD"], 16)
            cnt["LD"] += 16
            wait(q, "LD")

    def ffn_chunks(sink_tensor, bufs):
        sink = "X1"
        X1s = sink_tensor
        h1b, lorab, lorab2, wfcbuf, wpjbuf = bufs
        for cc in range(T // TCF):
            cs = slice(cc * TCF, (cc + 1) * TCF)

            @blk.tensor
            def _(tE):
                wait(tE, "V")
                for k in range(ND):
                    tE.matmul(psL16[:, 0:TCF],
                              afcB[:, k * 16:(k + 1) * 16],
                              xnb[:, k * T:(k + 1) * T][:, cs],
                              start=(k == 0), stop=(k == ND - 1))
                inc(tdrain(tE), "T")

            @blk.scalar
            def _(s):
                wait(s, "T")
                last = s.copy(lorab[:], psL16[:, 0:TCF])
                inc(last, "S")

            items = []
            for m in range(32):
                def mk_load(m=m):
                    def f(q):
                        q.dma_start(out=wfcbuf[:, (m % 2) * 1024:(m % 2 + 1) * 1024],
                                    in_=d_wqfcB[m * 128:(m + 1) * 128, :]
                                    ).then_inc(sems["LD"], 16)
                        return 1
                    return f

                def mk_mm(m=m, cs=cs):
                    def f(tE, ps):
                        lastk = None
                        for k in range(ND):
                            lastk = tE.matmul(ps[:, 0:TCF],
                                      wfcbuf[:, (m % 2) * 1024 + k * 128:
                                             (m % 2) * 1024 + (k + 1) * 128],
                                      xnb[:, k * T:(k + 1) * T][:, cs],
                                      start=(k == 0),
                                      stop=(DBGNOLORA and k == ND - 1))
                        if DBGNOLORA:
                            return lastk
                        return tE.matmul(ps[:, 0:TCF],
                                         bfc2B[:, m * 128:(m + 1) * 128],
                                         lorab[:], start=False, stop=True)
                    return f

                def mk_ev(m=m):
                    def f(sE, ps):
                        return sE.activation(h1b[:, m * TCF:(m + 1) * TCF],
                                             ps[:, 0:TCF], AF.Gelu,
                                             bias=colsF[:, C_BFC + m:C_BFC + m + 1])
                    return f
                items.append({"mms": mk_mm(), "evac": mk_ev(), "load": mk_load()})
            mm_stage(items, [psA, psB])
            barrier()

            @blk.tensor
            def _(tE):
                for k in range(32):
                    tE.matmul(psL16[:, 0:TCF],
                              aprojB[:, k * 16:(k + 1) * 16],
                              h1b[:, k * TCF:(k + 1) * TCF],
                              start=(k == 0), stop=(k == 31))
                inc(tdrain(tE), "T")

            @blk.scalar
            def _(s):
                wait(s, "T")
                last = s.copy(lorab2[:], psL16[:, 0:TCF])
                inc(last, "S")

            items = []
            for m in range(ND):
                def mk_load(m=m):
                    def f(q):
                        q.dma_start(out=wpjbuf[:, (m % 2) * 4096:(m % 2 + 1) * 4096],
                                    in_=d_wqpjB[m * 128:(m + 1) * 128, :]
                                    ).then_inc(sems["LD"], 16)
                        return 1
                    return f

                def mk_mm(m=m):
                    def f(tE, ps):
                        lastk = None
                        for k in range(32):
                            lastk = tE.matmul(ps[:, 0:TCF],
                                      wpjbuf[:, (m % 2) * 4096 + k * 128:
                                             (m % 2) * 4096 + (k + 1) * 128],
                                      h1b[:, k * TCF:(k + 1) * TCF],
                                      start=(k == 0),
                                      stop=(DBGNOLORA and k == 31))
                        if DBGNOLORA:
                            return lastk
                        return tE.matmul(ps[:, 0:TCF],
                                         bproj2B[:, m * 128:(m + 1) * 128],
                                         lorab2[:], start=False, stop=True)
                    return f

                def mk_ev(m=m, cs=cs):
                    def f(vE, ps):
                        xj = X1s[:, m * T:(m + 1) * T][:, cs]
                        return vE.scalar_tensor_tensor(
                            xj, ps[:, 0:TCF],
                            colsF[:, C_BPJ + m:C_BPJ + m + 1], xj,
                            ALU.add, ALU.add)
                    return f
                items.append({"mms": mk_mm(), "evac": mk_ev(), "load": mk_load()})
            mm_stage(items, [psC, psD], evac_eng="V")
            barrier()

    if upto == "p1dbg":
        x1_stack.close()
        ctx.close()
        return nc

    if upto == "p1":
        tap_X1_and_finish()
        x1_stack.close()
        ctx.close()
        return nc

    with ExitStack() as p2:
        pr = p2.enter_context
        with ExitStack() as lnscope:
            lr = lnscope.enter_context
            scr = (lr(nc.sbuf_tensor("f_xb", [128, ND * 512], BF16, side="right")),
                   lr(nc.sbuf_tensor("f_sq", [128, ND * 512], BF16, side="right")),
                   lr(nc.sbuf_tensor("f_m", [128, 512], F32, side="right")),
                   lr(nc.sbuf_tensor("f_r", [128, 512], F32, side="right")),
                   lr(nc.sbuf_tensor("f_t", [128, 512], F32, side="right")))
            emit_ln(lambda j, cs: X1[:, j * T:(j + 1) * T][:, cs], scr)
            barrier()
        bufs = (pr(nc.sbuf_tensor("f_h1", [128, 32 * TCF], BF16, side="right")),
                pr(nc.sbuf_tensor("f_lr", [16, TCF], BF16, side="right")),
                pr(nc.sbuf_tensor("f_lr2", [16, TCF], BF16, side="right")),
                pr(nc.sbuf_tensor("f_wfc", [128, 2 * 1024], BF16, side="right")),
                pr(nc.sbuf_tensor("f_wpj", [128, 2 * 4096], BF16, side="right")))
        ffn_chunks(X1, bufs)

    if upto == "p2":
        tap_X1_and_finish()
        x1_stack.close()
        ctx.close()
        return nc

    with ExitStack() as p3ln:
        lr = p3ln.enter_context
        scr = (lr(nc.sbuf_tensor("m_xb", [128, ND * 512], BF16, side="right")),
               lr(nc.sbuf_tensor("m_sq", [128, ND * 512], BF16, side="right")),
               lr(nc.sbuf_tensor("m_m", [128, 512], F32, side="right")),
               lr(nc.sbuf_tensor("m_r", [128, 512], F32, side="right")),
               lr(nc.sbuf_tensor("m_t", [128, 512], F32, side="right")))
        emit_ln(lambda j, cs: X1[:, j * T:(j + 1) * T][:, cs], scr)
        barrier()

        @blk.sync
        def _(q):
            for j in range(ND):
                q.dma_start(out=d_xspill[j * 128:(j + 1) * 128, :],
                            in_=X1[:, j * T:(j + 1) * T]).then_inc(sems["LD"], 16)
            cnt["LD"] += 16 * ND
        barrier()
    x1_stack.close()

    with ExitStack() as p3:
        pr = p3.enter_context
        xibs = [pr(nc.sbuf_tensor(f"xib{p}", [128, NDI * (TC + 8)], BF16,
                                   side="right")) for p in range(2)]
        gbs = [pr(nc.sbuf_tensor(f"gb{p}", [128, NDI * TC], BF16,
                                 side="right")) for p in range(2)]
        xcbs = [pr(nc.sbuf_tensor(f"xcb{p}", [128, NDI * TC], BF16,
                                  side="right")) for p in range(2)]
        breps = [pr(nc.sbuf_tensor(f"brep{p}", [128, NS * TC], BF16,
                                   side="right")) for p in range(2)]
        creps = [pr(nc.sbuf_tensor(f"crep{p}", [128, NS * TC], BF16,
                                   side="right")) for p in range(2)]
        HH = 4
        TCH = TC + HH
        hgb = pr(nc.sbuf_tensor("hgb", [128, SG * TCH], F32, side="right"))
        hcg = pr(nc.sbuf_tensor("hcg", [128, SG * TC], F32, side="right"))
        treb = pr(nc.sbuf_tensor("treb", [128, 2 * TC], F32, side="right"))
        delb = pr(nc.sbuf_tensor("delb", [128, 2 * TC], F32, side="right"))
        dub = pr(nc.sbuf_tensor("dub", [128, TC], F32, side="right"))
        dab = pr(nc.sbuf_tensor("dab", [128, 2 * TCH], F32, side="right"))
        dbub = pr(nc.sbuf_tensor("dbub", [128, NS * TCH], F32, side="right"))
        carrB = pr(nc.sbuf_tensor("carrB", [128, NDI * NS * HH], F32,
                                  side="right"))
        yb = pr(nc.sbuf_tensor("yb", [128, TC], F32, side="right"))
        ydb = pr(nc.sbuf_tensor("ydb", [128, TC], F32, side="right"))
        dtbufs = [pr(nc.sbuf_tensor(f"dtbuf{p}", [64, TC], BF16,
                                    side="right")) for p in range(2)]
        bcbs = [pr(nc.sbuf_tensor(f"bcb{p}", [16, 2 * TC], BF16,
                                  side="right")) for p in range(2)]
        spt = pr(nc.sbuf_tensor("spt", [128, 4 * TC], F32, side="right"))
        inwbuf = pr(nc.sbuf_tensor("inwbuf", [128, 2 * 1024], BF16, side="right"))
        opwbuf = pr(nc.sbuf_tensor("opwbuf", [128, 2 * 2048], BF16, side="right"))
        xpwb = pr(nc.sbuf_tensor("xpwb", [128, NDI * 96], BF16, side="right"))
        dtwb = pr(nc.sbuf_tensor("dtwb", [64, DI], BF16, side="right"))
        xstr = pr(nc.sbuf_tensor("xstr", [128, ND * TC], F32, side="right"))

        nlayer_run = _NL[0] if upto == "p3dbg" else NLAYER
        for layer in range(nlayer_run):
            is_last = layer == nlayer_run - 1

            @blk.sync
            def _(q):
                q.dma_start(out=cwF[:], in_=d_cw[layer]).then_inc(sems["LD"], 16)
                q.dma_start(out=colsF[:, C_CB:C_CB + 16],
                            in_=d_cb[layer]).then_inc(sems["LD"], 16)
                q.dma_start(out=colsF[:, C_DTB:C_DTB + 16],
                            in_=d_dtb[layer]).then_inc(sems["LD"], 16)
                q.dma_start(out=colsF[:, C_DSK:C_DSK + 16],
                            in_=d_dsk[layer]).then_inc(sems["LD"], 16)
                q.dma_start(out=acolF[:], in_=d_alog[layer]).then_inc(sems["LD"], 16)
                for k in range(NDI):
                    q.dma_start(out=xpwb[:, k * 96:(k + 1) * 96],
                                in_=d_xpwB[layer][k * 128:(k + 1) * 128, :]
                                ).then_inc(sems["LD"], 16)
                q.dma_start(out=dtwb[:], in_=d_dtwB[layer]).then_inc(sems["LD"], 16)
                cnt["LD"] += 16 * (6 + NDI)

            @blk.scalar
            def _(s):
                wait(s, "LD")
                last = s.activation(acolF[:], acolF[:], AF.Exp)
                inc(last, "S")

            @blk.vector
            def _(v):
                wait(v, "S")
                v.tensor_scalar(acolF[:], acolF[:], -1.0, None, ALU.mult)
                v.memset(xibs[0][:], 0.0)
                v.memset(xibs[1][:], 0.0)
                v.memset(dbub[:], 0.0)
                v.memset(carrB[:], 0.0)
                last = v.memset(dab[:], 0.0)
                inc(last, "V")
            barrier()

            R = [dict() for _ in range(NCH)]

            def emit_front_a(i):
                pv = i % 2
                cs = slice(i * TC, (i + 1) * TC)
                if i > 0:
                    ev_prev = R[i - 1]["e0c1"] + 16

                    @blk.vector
                    def _(v):
                        v.wait_ge(sems["S"], ev_prev)
                        for j in range(NDI):
                            b0 = j * (TC + 8)
                            dsth = xibs[pv][:, b0:b0 + 8].bitcast(
                                mybir.dt.uint32)
                            srch = xibs[1 - pv][:, b0 + TC:b0 + TC + 8].bitcast(
                                mybir.dt.uint32)
                            last = v.tensor_tensor(dsth, srch, srch,
                                                   ALU.bitwise_or)
                        inc(last, "V")

                R[i]["e0c1"] = cnt["S"]
                items = []
                for m in range(32):
                    def mk_load(layer=layer, m=m):
                        def f(q):
                            q.dma_start(out=inwbuf[:, (m % 2) * 1024:(m % 2 + 1) * 1024],
                                        in_=d_inwB[layer][m * 128:(m + 1) * 128, :]
                                        ).then_inc(sems["LD"], 16)
                            return 1
                        return f

                    def mk_mm(m=m, cs=cs):
                        def f(tE, ps):
                            for k in range(ND):
                                last = tE.matmul(
                                    ps[:, 0:TC],
                                    inwbuf[:, (m % 2) * 1024 + k * 128:
                                           (m % 2) * 1024 + (k + 1) * 128],
                                    xnb[:, k * T:(k + 1) * T][:, cs],
                                    start=(k == 0), stop=(k == ND - 1))
                            return last
                        return f

                    if m < 16:
                        def mk_ev(m=m, pv=pv):
                            def f(sE, ps):
                                b0 = m * (TC + 8)
                                return sE.copy(xibs[pv][:, b0 + 8:b0 + 8 + TC],
                                               ps[:, 0:TC])
                            return f
                    else:
                        def mk_ev(m=m, pv=pv):
                            def f(sE, ps):
                                return sE.activation(
                                    gbs[pv][:, (m - 16) * TC:(m - 15) * TC],
                                    ps[:, 0:TC], AF.Silu)
                            return f
                    items.append({"mms": mk_mm(), "evac": mk_ev(),
                                  "load": mk_load()})
                ent = ((R[i - 2]["s_c6ev"], R[i - 2]["v_c6ev"])
                       if i >= 2 else (0, 0))
                pw = ((("V", R[i - 2]["vprot_last"]),) if i >= 2 else ())
                mm_stage(items, [psA, psB], entry=ent, evac_prewait=pw)

            def emit_front_b(i):
                pv = i % 2
                e0c1 = R[i]["e0c1"]
                v0c2 = cnt["V"]

                @blk.vector
                def _(v):
                    for j in range(NDI):
                        b0 = j * (TC + 8)
                        v.wait_ge(sems["S"],
                                  e0c1 + (j + 1 if j < 2 else j + 31))
                        sl = spt[:, (j % 2) * TC:(j % 2 + 1) * TC]
                        v.tensor_scalar(sl, xibs[pv][:, b0 + 5:b0 + 5 + TC],
                                        cwF[:, j * 4:j * 4 + 1], None, ALU.mult)
                        for tp in range(1, 4):
                            last = v.scalar_tensor_tensor(
                                sl, xibs[pv][:, b0 + 5 + tp:b0 + 5 + tp + TC],
                                cwF[:, j * 4 + tp:j * 4 + tp + 1], sl,
                                ALU.mult, ALU.add)
                        last.then_inc(sems["V"], 1)
                        cnt["V"] += 1

                tc6 = R[i - 2]["t_c6"] if i >= 2 else 0

                @blk.scalar
                def _(s):
                    for j in range(NDI):
                        if j == 0 and tc6:
                            s.wait_ge(sems["T"], tc6)
                        s.wait_ge(sems["V"], v0c2 + j + 1)
                        last = s.activation(xcbs[pv][:, j * TC:(j + 1) * TC],
                                            spt[:, (j % 2) * TC:(j % 2 + 1) * TC],
                                            AF.Silu,
                                            bias=colsF[:, C_CB + j:C_CB + j + 1])
                        last.then_inc(sems["S"], 1)
                        cnt["S"] += 1

            def emit_front_c(i):
                pv = i % 2
                s_silu = R[i]["e0c1"] + 48

                @blk.tensor
                def _(tE):
                    tE.wait_ge(sems["S"], s_silu)
                    for k in range(NDI):
                        tE.matmul(psS96[0:64, 0:TC],
                                  xpwb[:, k * 96:k * 96 + 64],
                                  xcbs[pv][:, k * TC:(k + 1) * TC],
                                  start=(k == 0), stop=(k == NDI - 1))
                    for k in range(NDI):
                        tE.matmul(psL16[:, 0:TC],
                                  xpwb[:, k * 96 + 64:k * 96 + 80],
                                  xcbs[pv][:, k * TC:(k + 1) * TC],
                                  start=(k == 0), stop=(k == NDI - 1))
                    for k in range(NDI):
                        tE.matmul(psL16b[:, 0:TC],
                                  xpwb[:, k * 96 + 80:k * 96 + 96],
                                  xcbs[pv][:, k * TC:(k + 1) * TC],
                                  start=(k == 0), stop=(k == NDI - 1))
                    inc(tdrain(tE), "T")

                @blk.scalar
                def _(s):
                    wait(s, "T")
                    s.copy(dtbufs[pv][:], psS96[0:64, 0:TC])
                    s.copy(bcbs[pv][:, 0:TC], psL16[:, 0:TC])
                    last = s.copy(bcbs[pv][:, TC:2 * TC], psL16b[:, 0:TC])
                    inc(last, "S")
                R[i]["s_xpev"] = cnt["S"]

                items = []
                for which in range(2):
                    dstb = breps[pv] if which == 0 else creps[pv]
                    for s_ in range(NS):
                        def mk_mm(s_=s_, which=which, pv=pv):
                            def f(tE, ps):
                                return tE.matmul(ps[:, 0:TC],
                                                 selB[:, s_ * 128:(s_ + 1) * 128],
                                                 bcbs[pv][:, which * TC:(which + 1) * TC],
                                                 start=True, stop=True)
                            return f

                        def mk_ev(s_=s_, dstb=dstb):
                            def f(sE, ps):
                                return sE.copy(dstb[:, s_ * TC:(s_ + 1) * TC],
                                               ps[:, 0:TC])
                            return f
                        items.append({"mms": mk_mm(), "evac": mk_ev()})
                pw = ((("V", R[i - 2]["vprot_last"]),) if i >= 2 else ())
                mm_stage(items, [psA, psB], evac_prewait=pw)
                R[i]["s_repev"] = cnt["S"]

            def emit_sb_dtile(cc, j):
                pv = cc % 2
                pj = j % 2
                vprot = R[cc].setdefault("vprot", {})

                @blk.tensor
                def _(tE):
                    if j >= 2:
                        wait(tE, "V", vprot[j - 2])
                    else:
                        if cc >= 1:
                            tE.wait_ge(sems["S"], R[cc - 1]["s_sp15"])
                        else:
                            tE.wait_ge(sems["S"], R[0]["s_xpev"])
                    tE.matmul((psC if pj == 0 else psD)[:, 0:TC],
                              dtwb[:, j * 128:(j + 1) * 128],
                              dtbufs[pv][:], start=True, stop=True)
                    inc(tdrain(tE), "T")

                @blk.scalar
                def _(s):
                    wait(s, "T")
                    if j >= 1:
                        wait(s, "V", vprot[j - 1])
                    elif cc >= 1:
                        s.wait_ge(sems["V"], R[cc - 1]["vdel15"])
                    ps = (psC if pj == 0 else psD)
                    sp0 = spt[:, 2 * TC:3 * TC]
                    sp1 = spt[:, 3 * TC:4 * TC]
                    s.activation(sp0, ps[:, 0:TC], AF.Abs,
                                 bias=colsF[:, C_DTB + j:C_DTB + j + 1])
                    s.activation(sp1, sp0, AF.Exp,
                                 scale=colsF[:, C_NEG1:C_NEG1 + 1])
                    s.activation(sp0, sp1, AF.Ln,
                                 bias=colsF[:, C_ONE:C_ONE + 1])
                    last = s.activation(sp1, ps[:, 0:TC], AF.Relu,
                                        bias=colsF[:, C_DTB + j:C_DTB + j + 1])
                    inc(last, "S")
                if j == NDI - 1:
                    R[cc]["s_sp15"] = cnt["S"]

                @blk.vector
                def _(v):
                    wait(v, "S")
                    dj = delb[:, pj * TC:(pj + 1) * TC]
                    v.tensor_tensor(dj, spt[:, 2 * TC:3 * TC],
                                    spt[:, 3 * TC:4 * TC], ALU.add)
                    last = v.tensor_tensor(dub[:], dj,
                                           xcbs[pv][:, j * TC:(j + 1) * TC],
                                           ALU.mult)
                    inc(last, "V")
                if j == NDI - 1:
                    R[cc]["vdel15"] = cnt["V"]

                s1 = cnt["S"]
                v1 = cnt["V"]

                @blk.scalar
                def _(s):
                    for si in range(NS):
                        if si < 2:
                            s.wait_ge(sems["V"], v1)
                        else:
                            s.wait_ge(sems["V"], v1 + si)
                        last = s.activation(
                            dab[:, (si % 2) * TCH + HH:(si % 2 + 1) * TCH],
                            delb[:, pj * TC:(pj + 1) * TC], AF.Exp,
                            scale=acolF[:, j * NS + si:j * NS + si + 1])
                        last.then_inc(sems["S"], 1)
                        cnt["S"] += 1

                @blk.vector
                def _(v):
                    dbu_out = dbub[:].rearrange(
                        "p (s t) -> p s t", s=NS)[:, :, HH:TCH]
                    dub_b = dub[:].unsqueeze(1).broadcast_to([128, NS, TC])
                    brep_v = breps[pv][:].rearrange("p (s t) -> p s t", s=NS)
                    v.tensor_tensor(dbu_out, dub_b, brep_v, ALU.mult)
                    hdr_dst = dbub[:].rearrange(
                        "p (s t) -> p s t", s=NS)[:, :, 0:HH]
                    hdr_src = carrB[:, j * NS * HH:(j + 1) * NS * HH
                                    ].rearrange("p (s h) -> p s h", s=NS)
                    last = v.tensor_tensor(hdr_dst, hdr_src, hdr_src,
                                           ALU.max)
                    last.then_inc(sems["V"], 1)
                    cnt["V"] += 1
                    v.wait_ge(sems["V"], cnt["V"])
                    for si in range(NS):
                        v.wait_ge(sems["S"], s1 + si + 1)
                        sseg = si * TCH
                        gslot = si % SG
                        gseg = gslot * TCH
                        last = v.tensor_tensor_scan(
                            hgb[:, gseg:gseg + TCH],
                            dab[:, (si % 2) * TCH:(si % 2 + 1) * TCH],
                            dbub[:, sseg:sseg + TCH], 0.0,
                            ALU.mult, ALU.add)
                        last.then_inc(sems["V"], 1)
                        cnt["V"] += 1
                        if gslot == SG - 1:
                            v.wait_ge(sems["V"], cnt["V"])
                            g0 = si - SG + 1
                            hsl = hgb[:].rearrange(
                                "p (s t) -> p s t", s=SG)[:, :, TCH - HH:TCH]
                            dsl = carrB[:, (j * NS + g0) * HH:
                                        (j * NS + g0 + SG) * HH].rearrange(
                                "p (s h) -> p s h", s=SG)
                            v.tensor_tensor(dsl, hsl, hsl, ALU.max)
                            hgd = hgb[:].rearrange(
                                "p (s t) -> p s t", s=SG)[:, :, HH:TCH]
                            v.tensor_tensor(
                                hcg[:].rearrange("p (s t) -> p s t", s=SG),
                                hgd,
                                creps[pv][:, g0 * TC:(g0 + SG) * TC].rearrange(
                                    "p (s t) -> p s t", s=SG),
                                ALU.mult)
                            v.tensor_tensor(treb[:], hcg[:, 0:2 * TC],
                                            hcg[:, 2 * TC:4 * TC], ALU.add)
                            if g0 == 0:
                                v.tensor_tensor(yb[:], treb[:, 0:TC],
                                                treb[:, TC:2 * TC], ALU.add)
                            else:
                                v.tensor_tensor(treb[:, 0:TC], treb[:, 0:TC],
                                                treb[:, TC:2 * TC], ALU.add)
                                v.tensor_tensor(yb[:], yb[:],
                                                treb[:, 0:TC], ALU.add)
                    v.scalar_tensor_tensor(ydb[:], xcbs[pv][:, j * TC:(j + 1) * TC],
                                           colsF[:, C_DSK + j:C_DSK + j + 1],
                                           yb[:], ALU.mult, ALU.add)
                    last = v.tensor_tensor(xcbs[pv][:, j * TC:(j + 1) * TC],
                                           ydb[:], gbs[pv][:, j * TC:(j + 1) * TC],
                                           ALU.mult)
                    last.then_inc(sems["V"], 1)
                    cnt["V"] += 1
                vprot[j] = cnt["V"]
                if j == NDI - 1:
                    R[cc]["vprot_last"] = cnt["V"]

            def emit_c6(cc):
                pv = cc % 2
                cs = slice(cc * TC, (cc + 1) * TC)
                ldx0 = cnt["LD"]
                if is_last:
                    @blk.sync
                    def _(q):
                        wait(q, "GD")
                        for m in range(ND):
                            q.dma_start(out=xstr[:, m * TC:(m + 1) * TC],
                                        in_=d_xspill[m * 128:(m + 1) * 128, :][:, cs]
                                        ).then_inc(sems["LD"], 16)
                        cnt["LD"] += 16 * ND

                items = []
                for m in range(ND):
                    def mk_load(layer=layer, m=m):
                        def f(q):
                            q.dma_start(out=opwbuf[:, (m % 2) * 2048:(m % 2 + 1) * 2048],
                                        in_=d_opwB[layer][m * 128:(m + 1) * 128, :]
                                        ).then_inc(sems["LD"], 16)
                            return 1
                        return f

                    def mk_mm(m=m, pv=pv):
                        def f(tE, ps):
                            for k in range(NDI):
                                last = tE.matmul(
                                    ps[:, 0:TC],
                                    opwbuf[:, (m % 2) * 2048 + k * 128:
                                           (m % 2) * 2048 + (k + 1) * 128],
                                    xcbs[pv][:, k * TC:(k + 1) * TC],
                                    start=(k == 0), stop=(k == NDI - 1))
                            return last
                        return f

                    if not is_last:
                        def mk_ev(m=m, cs=cs):
                            def f(sE, ps):
                                return sE.copy(xnb[:, m * T:(m + 1) * T][:, cs],
                                               ps[:, 0:TC])
                            return f
                    else:
                        def mk_ev(m=m):
                            def f(vE, ps):
                                return vE.tensor_tensor(
                                    xstr[:, m * TC:(m + 1) * TC],
                                    xstr[:, m * TC:(m + 1) * TC],
                                    ps[:, 0:TC], ALU.add)
                            return f
                    items.append({"mms": mk_mm(), "evac": mk_ev(),
                                  "load": mk_load()})
                if not is_last:
                    mm_stage(items, [psA, psB],
                             entry=(cnt["S"], R[cc]["vprot_last"]))
                else:
                    ldx1 = ldx0 + 16 * ND
                    mm_stage(items, [psA, psB], evac_eng="V",
                             entry=(cnt["S"], R[cc]["vprot_last"]),
                             evac_prewait=(("LD", ldx1),))
                R[cc]["t_c6"] = cnt["T"]
                R[cc]["s_c6ev"] = cnt["S"]
                R[cc]["v_c6ev"] = cnt["V"] if is_last else 0

                if is_last:
                    vv0 = cnt["V"] - ND

                    @blk.gpsimd
                    def _(g):
                        for m in range(ND):
                            g.wait_ge(sems["V"], vv0 + m + 1)
                            g.dma_start(out=d_xafter[m * 128:(m + 1) * 128, :][:, cs],
                                        in_=xstr[:, m * TC:(m + 1) * TC]
                                        ).then_inc(sems["GD"], 16)
                        cnt["GD"] += 16 * ND

            for i in range(NCH + 2):
                if 1 <= i <= NCH:
                    emit_sb_dtile(i - 1, 0)
                if 2 <= i:
                    emit_c6(i - 2)
                if i < NCH:
                    emit_front_a(i)
                    emit_front_b(i)
                    emit_front_c(i)
                if 1 <= i <= NCH:
                    for j in range(1, NDI):
                        emit_sb_dtile(i - 1, j)
            barrier()

        if upto == "p3dbg":
            @blk.sync
            def _(q):
                for nm in ("T", "S", "V", "GD", "SD"):
                    q.wait_ge(sems[nm], cnt[nm])
                q.dma_start(out=d_y[512:640, 0:256], in_=acolF[:]).then_inc(sems["LD"], 16)
                q.dma_start(out=d_y[768:896, 0:1040], in_=hgb[:]).then_inc(sems["LD"], 16)
                q.dma_start(out=d_y[640:768, 0:512], in_=delb[:]).then_inc(sems["LD"], 16)
                cnt["LD"] += 16 * 3
                wait(q, "LD")

            @blk.gpsimd
            def _(g):
                wait(g, "LD")
                g.dma_start(out=d_y[0:128, :], in_=gbs[1][:, 0:2048]).then_inc(sems["GD"], 16)
                g.dma_start(out=d_y[128:256, :], in_=xcbs[1][:, 0:2048]).then_inc(sems["GD"], 16)
                g.dma_start(out=d_y[256:384, :], in_=xibs[1][:, 0:2048]).then_inc(sems["GD"], 16)
                g.dma_start(out=d_y[448:464, 0:512], in_=bcbs[1][:]).then_inc(sems["GD"], 16)
                g.dma_start(out=d_y[384:448, 0:256], in_=dtbufs[1][:]).then_inc(sems["GD"], 16)
                g.dma_start(out=d_y[896:1024, :], in_=xnb[:, 0:2048]).then_inc(sems["GD"], 16)
                cnt["GD"] += 16 * 6
                wait(g, "GD")

    if upto == "p3dbg":
        ctx.close()
        return nc

    if upto == "p3":
        tap_xafter_and_finish()
        ctx.close()
        return nc

    with ExitStack() as p4:
        pr = p4.enter_context
        X4 = pr(nc.sbuf_tensor("X4", [128, ND * T], F32, side="right"))

        @blk.sync
        def _(q):
            wait(q, "GD")
            for j in range(ND):
                q.dma_start(out=X4[:, j * T:(j + 1) * T],
                            in_=d_xafter[j * 128:(j + 1) * 128, :]
                            ).then_inc(sems["LD"], 16)
            cnt["LD"] += 16 * ND
        barrier()

        with ExitStack() as lnscope:
            lr = lnscope.enter_context
            scr = (lr(nc.sbuf_tensor("g_xb", [128, ND * 512], BF16, side="right")),
                   lr(nc.sbuf_tensor("g_sq", [128, ND * 512], BF16, side="right")),
                   lr(nc.sbuf_tensor("g_m", [128, 512], F32, side="right")),
                   lr(nc.sbuf_tensor("g_r", [128, 512], F32, side="right")),
                   lr(nc.sbuf_tensor("g_t", [128, 512], F32, side="right")))
            emit_ln(lambda j, cs: X4[:, j * T:(j + 1) * T][:, cs], scr)
            barrier()

        bufs = (pr(nc.sbuf_tensor("g_h1", [128, 32 * TCF], BF16, side="right")),
                pr(nc.sbuf_tensor("g_lr", [16, TCF], BF16, side="right")),
                pr(nc.sbuf_tensor("g_lr2", [16, TCF], BF16, side="right")),
                pr(nc.sbuf_tensor("g_wfc", [128, 2 * 1024], BF16, side="right")),
                pr(nc.sbuf_tensor("g_wpj", [128, 2 * 4096], BF16, side="right")))
        ffn_chunks(X4, bufs)

        @blk.gpsimd
        def _(g):
            for nm in ("T", "S", "V", "LD", "SD"):
                g.wait_ge(sems[nm], cnt[nm])
            for j in range(ND):
                g.dma_start(out=d_y[j * 128:(j + 1) * 128, :],
                            in_=X4[:, j * T:(j + 1) * T]).then_inc(sems["GD"], 16)
            cnt["GD"] += 16 * ND
            wait(g, "GD")

    ctx.close()
    return nc


_CACHE = {}
_UPTO = ["full"]


def kernel(**inputs):
    x = np.asarray(inputs["L0_attn_a_x"])
    B = x.shape[0]

    def col(v, n):
        return np.ascontiguousarray(np.asarray(v).reshape(n, 128).T)

    shared = {
        "wz": np.ascontiguousarray(inputs["L0_attn_a_Wz"]),
        "wh": np.ascontiguousarray(inputs["L0_attn_a_Wh"]),
        "bz": col(inputs["L0_attn_a_bz"], ND),
        "bh": col(inputs["L0_attn_a_bh"], ND),
        "logd": col(inputs["L0_attn_a_log_D"], ND),
        "wfcT": np.ascontiguousarray(np.asarray(inputs["L0_ffn_W_fc"]).T),
        "wprojT": np.ascontiguousarray(np.asarray(inputs["L0_ffn_W_proj"]).T),
        "afcT": np.ascontiguousarray(np.asarray(inputs["L0_ffn_A_fc"]).T),
        "bfcT": np.ascontiguousarray(np.asarray(inputs["L0_ffn_B_fc"]).T),
        "aprojT": np.ascontiguousarray(np.asarray(inputs["L0_ffn_A_proj"]).T),
        "bprojT": np.ascontiguousarray(np.asarray(inputs["L0_ffn_B_proj"]).T),
        "bfc": col(inputs["L0_ffn_b_fc"], 32),
        "bproj": col(inputs["L0_ffn_b_proj"], ND),
    }
    for l in range(NLAYER):
        shared[f"inwT{l}"] = np.ascontiguousarray(
            np.asarray(inputs["L0_attn_b_in_proj_w"])[l].T)
        shared[f"xpwT{l}"] = np.ascontiguousarray(
            np.asarray(inputs["L0_attn_b_x_proj_w"])[l].T)
        shared[f"dtwT{l}"] = np.ascontiguousarray(
            np.asarray(inputs["L0_attn_b_dt_proj_w"])[l].T)
        shared[f"opwT{l}"] = np.ascontiguousarray(
            np.asarray(inputs["L0_attn_b_out_proj_w"])[l].T)
        cw = np.asarray(inputs["L0_attn_b_conv_w"])[l]
        shared[f"cw{l}"] = np.ascontiguousarray(
            cw.reshape(NDI, 128, 4).transpose(1, 0, 2).reshape(128, NDI * 4))
        shared[f"cb{l}"] = col(np.asarray(inputs["L0_attn_b_conv_b"])[l], NDI)
        shared[f"dtb{l}"] = col(np.asarray(inputs["L0_attn_b_dt_proj_b"])[l], NDI)
        al = np.asarray(inputs["L0_attn_b_A_log"])[l]
        shared[f"alog{l}"] = np.ascontiguousarray(
            al.reshape(NDI, 128, NS).transpose(1, 0, 2).reshape(128, NDI * NS))
        shared[f"dsk{l}"] = col(np.asarray(inputs["L0_attn_b_D"])[l], NDI)

    key = _UPTO[0]
    if key not in _CACHE:
        _CACHE[key] = build(key)
    nc = _CACHE[key]

    in_maps = []
    for i in range(B):
        m = {"x": np.ascontiguousarray(x[i].T)}
        m.update(shared)
        in_maps.append(m)
    res = run_bass_kernel_spmd(nc, in_maps, core_ids=list(range(B)))
    out = np.stack([np.asarray(r["y"]).T for r in res.results], axis=0)
    return out.astype(np.float32)
